# version 32
# baseline (speedup 1.0000x reference)
import numpy as np
import ml_dtypes
import concourse.bass as bass
import concourse.mybir as mybir
from concourse.bass_utils import run_bass_kernel_spmd

F32 = mybir.dt.float32
BF16 = mybir.dt.bfloat16
AF = mybir.ActivationFunctionType
ALU = mybir.AluOpType
AX = mybir.AxisListType

SEQ, DM, NH, HD, DROPE, LAT = 2048, 1024, 16, 64, 16, 256
NP = 128
NST = SEQ // NP
NDT = DM // NP
NPAIR = NH // 2
SQW = 1024
NSQC = SEQ // SQW
NSP = NSQC * NPAIR
EPS = 1e-5
KB = 1024


def build_program():
    nc = bass.Bass("TRN2", target_bir_lowering=False)

    x_h = nc.dram_tensor("x", [SEQ, DM], F32, kind="ExternalInput")
    wq_h = nc.dram_tensor("wq", [DM, DM], F32, kind="ExternalInput")
    wka_h = nc.dram_tensor("wka", [LAT, DM], F32, kind="ExternalInput")
    wkb_h = nc.dram_tensor("wkb", [DM, LAT], F32, kind="ExternalInput")
    wva_h = nc.dram_tensor("wva", [LAT, DM], F32, kind="ExternalInput")
    wvb_h = nc.dram_tensor("wvb", [DM, LAT], F32, kind="ExternalInput")
    wo_h = nc.dram_tensor("wo", [DM, DM], F32, kind="ExternalInput")
    wf_h = nc.dram_tensor("wf", [DM, DM], F32, kind="ExternalInput")
    alpha_h = nc.dram_tensor("alpha", [1, 1], F32, kind="ExternalInput")
    bias_h = nc.dram_tensor("fbias", [1, DM], F32, kind="ExternalInput")
    ident_h = nc.dram_tensor("identi", [NP, NP], BF16, kind="ExternalInput")
    rhi_h = nc.dram_tensor("rhi", [NP, NPAIR * NP], BF16, kind="ExternalInput")
    rlo_h = nc.dram_tensor("rlo", [NP, NPAIR * NP], BF16, kind="ExternalInput")
    out_h = nc.dram_tensor("out", [SEQ, DM], F32, kind="ExternalOutput")

    ARENA = 206 * KB
    arena = nc.alloc_sbuf_tensor("arena", [NP, ARENA], mybir.dt.uint8)
    arena_base = nc.lookup_mloc(arena).addr

    def place(name, shape, dtype, off):
        per_part = int(np.prod(shape[1:])) * mybir.dt.size(dtype)
        assert off % 32 == 0 and off + per_part <= ARENA, (name, off, per_part)
        h = nc.virtual_sbuf_tensor(name, list(shape), dtype)
        nc._tensor(h.name, list(h.shape), h.dtype, type="SB")
        mloc = nc.lookup_mloc(h)
        mloc.allocated = not nc.target_bir_lowering
        mloc.addr = arena_base + off
        return h

    ident = place("identb", [NP, NP], BF16, 0)
    ones_den = place("ones_den", [NP, HD], BF16, 256)
    onesf = place("onesf", [1, NP], F32, 384)
    vec = place("vec", [NP, 64], F32, 1 * KB)
    bias_row = place("bias_row", [1, DM], F32, 1 * KB + 256)
    alpha_sb = place("alpha_sb", [1, 1], F32, 5 * KB + 256)
    stat_sb = place("stat_sb", [1, 8], F32, 5 * KB + 288)
    rd_bc = place("rd_bc", [NP, SQW], F32, 5 * KB + 512)
    x1bfw = place("x1bfw", [NP, 2 * DM], BF16, 5 * KB + 512)
    ones2 = place("ones2", [2, NP], BF16, 9 * KB + 512)
    bias_hl = place("bias_hl", [2, DM], BF16, 10 * KB)
    bias_lo = place("bias_lo", [1, DM], BF16, 12 * KB)
    woT = place("woT", [NP, NDT * DM], BF16, 14 * KB)
    wqstage = place("wqstage", [NP, NDT * DM], BF16, 78 * KB)
    xTb = place("xTb", [NP, 4 * SEQ], BF16, 30 * KB)
    wtT = place("wtT", [NP, NDT * DM], BF16, 30 * KB)
    wfa = place("wfa", [NP, 4 * DM], F32, 142 * KB)
    wfb = place("wfb", [NP, 2 * DM], F32, 174 * KB)
    wfc = place("wfc", [NP, 2 * DM], F32, 110 * KB)
    akT = place("akT", [NP, NDT * 2 * NP], BF16, 46 * KB)
    avT = place("avT", [NP, NDT * 2 * NP], BF16, 50 * KB)
    bkT = place("bkT", [NP, 2 * DM], BF16, 54 * KB)
    bvT = place("bvT", [NP, 2 * DM], BF16, 58 * KB)
    qTb = place("qTb", [NP, 2 * SEQ], BF16, 46 * KB)
    pring1 = place("pring1", [NP, 4 * SQW], BF16, 54 * KB)
    x1Ta = place("x1Ta", [NP, 4 * SEQ], BF16, 46 * KB)
    wostage = place("wostage", [NP, NDT * DM], BF16, 62 * KB)
    wqT = place("wqT", [NP, NDT * DM], BF16, 62 * KB)
    vfa = place("vfa", [NP, 8 * DM], BF16, 62 * KB)
    x1Tb = place("x1Tb", [NP, 4 * SEQ], BF16, 62 * KB)
    kT = place("kT", [NP, NDT * SEQ], BF16, 78 * KB)
    x1a = place("x1a", [NP, 8 * DM], F32, 78 * KB)
    wkbstage = place("wkbstage", [NP, 2 * DM], BF16, 94 * KB)
    wvbstage = place("wvbstage", [NP, 2 * DM], BF16, 98 * KB)
    kqstage = place("kqstage", [NP, 4 * DM], BF16, 110 * KB)
    pvden_stage = place("pvden_stage", [NP, 2 * SQW], F32, 110 * KB)
    wttmp = place("wttmp", [NP, 2 * DM], BF16, 198 * KB)
    wttmp2 = place("wttmp2", [NP, DM], BF16, 202 * KB)
    qTa = place("qTa", [NP, 6 * SEQ], BF16, 118 * KB)
    x1b = place("x1b", [NP, 8 * DM], F32, 110 * KB)
    xbf = place("xbf", [NP, NST * DM], BF16, 142 * KB)
    rhi_sb = place("rhi_sb", [NP, NPAIR * NP], BF16, 118 * KB)
    rlo_sb = place("rlo_sb", [NP, NPAIR * NP], BF16, 120 * KB)
    klT = place("klT", [NP, 2 * SEQ], BF16, 158 * KB)
    vlT = place("vlT", [NP, 2 * SEQ], BF16, 166 * KB)
    OT = place("OT", [NP, NPAIR * SEQ], BF16, 142 * KB)
    ttile = place("ttile", [NP, 2 * DM], F32, 142 * KB)
    r2tile = place("r2tile", [NP, 2 * DM], F32, 150 * KB)
    pring0 = place("pring0", [NP, 4 * SQW], BF16, 174 * KB)
    xTa = place("xTa", [NP, 4 * SEQ], BF16, 182 * KB)
    vfb = place("vfb", [NP, 8 * DM], BF16, 182 * KB)
    xres = place("xres", [NP, 2 * DM], F32, 174 * KB)
    rtile = place("rtile", [NP, 3 * DM], F32, 182 * KB)
    sqs = place("sqs", [NP, DM], F32, 194 * KB)
    ostage = place("ostage", [NP, 2 * DM], F32, 198 * KB)

    def xT_sl(dt, col, width):
        t = xTa if dt < 4 else xTb
        return t[:, (dt % 4) * SEQ + col: (dt % 4) * SEQ + col + width]

    def qT_sl(p, p0, nparts, col, width):
        t, i = (qTa, p) if p < 6 else (qTb, p - 6)
        return t[p0:p0 + nparts, i * SEQ + col: i * SEQ + col + width]

    def v_sl(st, col, width):
        t = vfa if st < 8 else vfb
        return t[:, (st % 8) * DM + col: (st % 8) * DM + col + width]

    def x1_t(st):
        t = x1a if st < 8 else x1b
        return t[:, (st % 8) * DM: (st % 8) * DM + DM]

    def pring(h):
        return pring0 if h == 0 else pring1

    ps = nc.psum_tensor("ps", [NP, 8 * 512], F32).__enter__()
    psb = nc.virtual_psum_tensor("psb", [NP, 8 * 1024], BF16)
    nc._tensor(psb.name, list(psb.shape), psb.dtype, type="PSUM")
    _pm = nc.lookup_mloc(psb)
    _pm.allocated = not nc.target_bir_lowering
    _pm.addr = nc.lookup_mloc(ps).addr

    def bank(i, n=1):
        return ps[:, i * 512: (i + n) * 512]

    def bankb(i, n=1):
        return psb[:, i * 1024: (i + n) * 1024]

    cnt = {}
    sems = {}
    for n in ("pe", "dv", "ac", "gp",
              "sc0", "sc1", "ex0", "ex1", "pvs", "eps", "pvt",
              "womm", "lnr", "lnbf", "x1tr", "x1ev",
              "ffnm", "ffne", "ffnr", "ln2ap", "gpt"):
        sems[n] = nc.semaphore(n).__enter__()
        cnt[n] = 0
    NDMASEM = 8
    for q in ("dgp", "dsy"):
        for i in range(NDMASEM):
            n = f"{q}{i}"
            sems[n] = nc.semaphore(n).__enter__()
            cnt[n] = 0
    dma_rr = {"dgp": 0, "dsy": 0}

    def inc(instr, s, n=1):
        instr.then_inc(sems[s], n)
        cnt[s] += n
        return cnt[s]

    def w(eng, s, val):
        if val > 0:
            eng.wait_ge(sems[s], val)

    def dma(q, pool, dst, src, transpose=False):
        i = dma_rr[pool] % NDMASEM
        dma_rr[pool] += 1
        s = f"{pool}{i}"
        inc(q.dma_start(dst, src, transpose=transpose), s, 16)
        return (s, cnt[s])

    def wdma(eng, tok):
        eng.wait_ge(sems[tok[0]], tok[1])

    def dvo(v, emit):
        w(v, "dv", cnt["dv"])
        return inc(emit(), "dv")

    def aco(a, emit):
        w(a, "ac", cnt["ac"])
        return inc(emit(), "ac")

    def gpo(g, emit):
        w(g, "gp", cnt["gp"])
        return inc(emit(), "gp")

    def vs(i):
        return vec[:, i:i + 1]

    ALPHA_V, DELTAP, DELTAN, ONESC = vs(0), vs(1), vs(2), vs(3)
    FPART = vec[:, 4:12]
    FPSUM = vs(12)

    def lnv(st, k):
        i = 16 + (st % 2) * 8 + k
        return vec[:, i:i + 1]

    M = {}

    def transpose_job(name, stage, stage_base, n_ot, d_ext, dst_sl,
                      wait_tok=None):
        n_dt = d_ext // NP
        base_pe, base_ac = cnt["pe"], cnt["ac"]

        with nc.Block(name) as blk:
            @blk.tensor
            def _(t):
                if wait_tok is not None:
                    wdma(t, wait_tok)
                for j in range(n_dt):
                    if j >= 2:
                        w(t, "ac", base_ac + (j - 2) + 1)
                    bb = bankb(4 + 2 * (j % 2), 1)
                    for i in range(n_ot):
                        c0 = stage_base + i * d_ext + j * NP
                        mm = t.transpose(bb[:, i * NP:(i + 1) * NP],
                                         stage[:, c0:c0 + NP], ident[:])
                    inc(mm, "pe")

            @blk.scalar
            def _(s):
                for j in range(n_dt):
                    w(s, "pe", base_pe + j + 1)
                    w(s, "ac", cnt["ac"])
                    inc(s.copy(dst_sl(j),
                               bankb(4 + 2 * (j % 2), 1)[:, 0:n_ot * NP]), "ac")

    with nc.Block("p00", no_gpsimd_drain=True) as blk:
        @blk.scalar
        def _(s):
            from concourse.hw_specs import get_activation_tables
            names = list(get_activation_tables(nc.m.arch).keys())
            s.add_instruction(mybir.InstLoadActFuncSet(
                name=nc.get_next_instruction_name(),
                act_func_set_id=names.index("natural_log_exp_and_others"),
                ins=[], outs=[]))

        @blk.vector
        def _(v):
            dvo(v, lambda: v.memset(ones_den[:], 1.0))
            dvo(v, lambda: v.memset(onesf[:], 1.0))
            dvo(v, lambda: v.memset(ONESC, 1.0))
            dvo(v, lambda: v.memset(ones2[:], 1.0))
            M["dv_const"] = cnt["dv"]

        @blk.gpsimd
        def _(g):
            M["d_x"] = [
                dma(g, "dgp",
                    xbf[:, h * 8 * DM:(h + 1) * 8 * DM]
                    .rearrange("p (t d) -> p t d", d=DM),
                    x_h.ap().rearrange("(t p) d -> p t d", p=NP)[:, h * 8:
                                                                (h + 1) * 8, :])
                for h in range(2)]
            M["d_wka"] = dma(g, "dgp",
                             kqstage[:, 0:2 * DM].rearrange(
                                 "p (t d) -> p t d", d=DM),
                             wka_h.ap().rearrange("(t p) d -> p t d", p=NP))
            M["d_wva"] = dma(g, "dgp",
                             kqstage[:, 2 * DM:4 * DM].rearrange(
                                 "p (t d) -> p t d", d=DM),
                             wva_h.ap().rearrange("(t p) d -> p t d", p=NP))
            M["d_wkb"] = dma(g, "dgp",
                             wkbstage[:].rearrange("p (t d) -> p t d", d=LAT),
                             wkb_h.ap().rearrange("(t p) d -> p t d", p=NP))
            M["d_wvb"] = dma(g, "dgp",
                             wvbstage[:].rearrange("p (t d) -> p t d", d=LAT),
                             wvb_h.ap().rearrange("(t p) d -> p t d", p=NP))
            M["d_wo"] = dma(g, "dgp",
                            wostage[:].rearrange("p (t d) -> p t d", d=DM),
                            wo_h.ap().rearrange("(t p) d -> p t d", p=NP))
            M["d_wq"] = dma(g, "dgp",
                            wqstage[:].rearrange("p (t d) -> p t d", d=DM),
                            wq_h.ap().rearrange("(t p) d -> p t d", p=NP))

        @blk.sync
        def _(sy):
            t1 = dma(sy, "dsy", ident[:], ident_h.ap())
            t2 = dma(sy, "dsy", alpha_sb[:], alpha_h.ap())
            t3 = dma(sy, "dsy", bias_row[:], bias_h.ap())
            M["d_small"] = [t1, t2, t3]
            M["d_R"] = [dma(sy, "dsy", rhi_sb[:], rhi_h.ap()),
                        dma(sy, "dsy", rlo_sb[:], rlo_h.ap())]

    with nc.Block("p1_xT") as blk:
        base_pe, base_dv = cnt["pe"], cnt["dv"]

        @blk.tensor
        def _(t):
            for h in range(2):
                wdma(t, M["d_x"][h])
                if h == 0:
                    for tok in M["d_small"]:
                        wdma(t, tok)
                for dt in range(NDT):
                    if dt >= 2:
                        w(t, "dv", base_dv + h * NDT + (dt - 2) + 1)
                    bb = bankb(4 + 2 * (dt % 2) + h, 1)
                    for j in range(8):
                        st = h * 8 + j
                        mm = t.transpose(
                            bb[:, j * NP:(j + 1) * NP],
                            xbf[:, st * DM + dt * NP: st * DM + (dt + 1) * NP],
                            ident[:])
                    inc(mm, "pe")

        @blk.vector
        def _(v):
            for h in range(2):
                for dt in range(NDT):
                    w(v, "pe", base_pe + h * NDT + dt + 1)
                    dvo(v, lambda h=h, dt=dt: v.tensor_copy(
                        xT_sl(dt, h * 1024, 1024),
                        bankb(4 + 2 * (dt % 2) + h, 1)))

    wf8 = wf_h.ap().rearrange("(t p) d -> p t d", p=NP)
    with nc.Block("p1_tab") as blk:
        @blk.sync
        def _(sy):
            M["d_wfa"] = dma(sy, "dsy",
                             wfa[:].rearrange("p (t d) -> p t d", d=DM),
                             wf8[:, 0:4, :])

        @blk.vector
        def _(v):
            wdma(v, M["d_small"][2])
            dvo(v, lambda: v.tensor_copy(bias_hl[0:1, :], bias_row[:]))
            dvo(v, lambda: v.tensor_tensor(bias_lo[:], bias_row[:],
                                           bias_hl[0:1, :], ALU.subtract))
            M["dv_blo"] = cnt["dv"]

        @blk.gpsimd
        def _(g):
            w(g, "dv", M["dv_blo"])
            M["d_blo"] = dma(g, "dgp", bias_hl[1:2, :], bias_lo[:])

    with nc.Block("p1_jobs") as blk:
        P0, A0, D0 = cnt["pe"], cnt["ac"], cnt["dv"]
        pe_off, ac_off = {}, {}
        _o = P0
        for nm, n in (("akT", 8), ("avT", 8), ("klT", 8), ("vlT", 8),
                      ("rkf", 8), ("bkT", 2), ("bvT", 2), ("woT", 8),
                      ("rqf", 16), ("wqT", 8)):
            pe_off[nm] = _o
            _o += n
        _o = A0
        for nm, n in (("akT", 8), ("avT", 8), ("rkf", 8), ("bkT", 2),
                      ("bvT", 2), ("woT", 8), ("rqf", 16), ("wqT", 8)):
            ac_off[nm] = _o
            _o += n
        dv_off = {"klT": D0, "vlT": D0 + 8}

        TJOBS = {
            "akT": (kqstage, 0, 2, DM,
                    lambda j: akT[:, j * 2 * NP:(j + 1) * 2 * NP]),
            "avT": (kqstage, 2 * DM, 2, DM,
                    lambda j: avT[:, j * 2 * NP:(j + 1) * 2 * NP]),
            "bkT": (wkbstage, 0, NDT, LAT,
                    lambda j: bkT[:, j * DM:(j + 1) * DM]),
            "bvT": (wvbstage, 0, NDT, LAT,
                    lambda j: bvT[:, j * DM:(j + 1) * DM]),
            "woT": (wostage, 0, NDT, DM,
                    lambda j: woT[:, j * DM:(j + 1) * DM]),
            "wqT": (wqstage, 0, NDT, DM,
                    lambda j: wqT[:, j * DM:(j + 1) * DM]),
        }
        LJOBS = {"klT": (akT, klT), "vlT": (avT, vlT)}
        RJOBS = {"rkf": (wkbstage, LAT), "rqf": (wqstage, DM)}

        @blk.tensor
        def _(t):
            def tjob(nm, start_ac, toks):
                stage, base, n_ot, d_ext, _sl = TJOBS[nm]
                for tok in toks:
                    wdma(t, tok)
                w(t, "ac", start_ac)
                for j in range(d_ext // NP):
                    if j >= 2:
                        w(t, "ac", ac_off[nm] + (j - 2) + 1)
                    bb = bankb(4 + 2 * (j % 2), 1)
                    for i in range(n_ot):
                        c0 = base + i * d_ext + j * NP
                        mm = t.transpose(bb[:, i * NP:(i + 1) * NP],
                                         stage[:, c0:c0 + NP], ident[:])
                    inc(mm, "pe")

            def ljob(nm, start_ac, start_dv=0):
                aT, _dst = LJOBS[nm]
                w(t, "ac", start_ac)
                if start_dv:
                    w(t, "dv", start_dv)
                i = 0
                for lt in range(2):
                    for sc in range(4):
                        if i >= 4:
                            w(t, "dv", dv_off[nm] + i - 4 + 1)
                        for dt in range(NDT):
                            mm = t.matmul(bank(i % 4),
                                          aT[:, dt * 2 * NP + lt * NP:
                                             dt * 2 * NP + (lt + 1) * NP],
                                          xT_sl(dt, sc * 512, 512),
                                          start=(dt == 0),
                                          stop=(dt == NDT - 1))
                        inc(mm, "pe")
                        i += 1

            def rjob(nm, start_ac, start_dv, toks):
                stage, tw = RJOBS[nm]
                nch = max(tw // 512, 1)
                cw = min(tw, 512)
                for tok in toks:
                    wdma(t, tok)
                if start_ac:
                    w(t, "ac", start_ac)
                if start_dv:
                    w(t, "dv", start_dv)
                i = 0
                for ti in range(NDT):
                    for ch in range(nch):
                        if i >= 4:
                            w(t, "ac", ac_off[nm] + (i - 4) + 1)
                        sl = stage[:, ti * tw + ch * cw:
                                   ti * tw + (ch + 1) * cw]
                        t.matmul(bank(i % 4)[:, 0:cw],
                                 rhi_sb[:, ti * NP:(ti + 1) * NP], sl,
                                 start=True, stop=False)
                        mm = t.matmul(bank(i % 4)[:, 0:cw],
                                      rlo_sb[:, ti * NP:(ti + 1) * NP], sl,
                                      start=False, stop=True)
                        inc(mm, "pe")
                        i += 1

            tjob("akT", 0, [M["d_wka"]])
            tjob("avT", ac_off["avT"], [M["d_wva"]])
            ljob("klT", ac_off["avT"])
            ljob("vlT", ac_off["rkf"], D0 + 8)
            rjob("rkf", 0, D0 + 16, M["d_R"] + [M["d_wkb"]])
            tjob("bkT", ac_off["bkT"], [])
            tjob("bvT", ac_off["bvT"], [M["d_wvb"]])
            tjob("woT", ac_off["woT"], [M["d_wo"]])
            rjob("rqf", ac_off["bkT"], 0, [M["d_wq"]])
            tjob("wqT", ac_off["wqT"], [])

        @blk.scalar
        def _(s):
            def tev(nm):
                stage, base, n_ot, d_ext, dst_sl = TJOBS[nm]
                for j in range(d_ext // NP):
                    s.wait_ge(sems["pe"], pe_off[nm] + j + 1)
                    w(s, "ac", cnt["ac"])
                    assert inc(s.copy(
                        dst_sl(j),
                        bankb(4 + 2 * (j % 2), 1)[:, 0:n_ot * NP]),
                        "ac") == ac_off[nm] + j + 1

            def rev(nm):
                stage, tw = RJOBS[nm]
                nch = max(tw // 512, 1)
                cw = min(tw, 512)
                i = 0
                for ti in range(NDT):
                    for ch in range(nch):
                        s.wait_ge(sems["pe"], pe_off[nm] + i + 1)
                        w(s, "ac", cnt["ac"])
                        assert inc(s.copy(
                            stage[:, ti * tw + ch * cw:
                                  ti * tw + (ch + 1) * cw],
                            bank(i % 4)[:, 0:cw]), "ac") == ac_off[nm] + i + 1
                        i += 1

            tev("akT")
            tev("avT")
            rev("rkf")
            tev("bkT")
            tev("bvT")
            tev("woT")
            rev("rqf")
            tev("wqT")

        @blk.vector
        def _(v):
            for nm in ("klT", "vlT"):
                _aT, dst = LJOBS[nm]
                i = 0
                for lt in range(2):
                    for sc in range(4):
                        v.wait_ge(sems["pe"], pe_off[nm] + i + 1)
                        c2 = dvo(v, lambda dst=dst, lt=lt, sc=sc, i=i:
                                 v.tensor_copy(
                                     dst[:, lt * SEQ + sc * 512:
                                         lt * SEQ + (sc + 1) * 512],
                                     bank(i % 4)))
                        assert c2 == dv_off[nm] + i + 1
                        i += 1

    with nc.Block("p1_kq") as blk:
        base_pe, base_ac, base_dv = cnt["pe"], cnt["ac"], cnt["dv"]
        tiles = ([("k", b, c) for b in range(NPAIR) for c in range(4)] +
                 [("q", b, c) for b in range(NPAIR) for c in range(4)])
        NTT = len(tiles)
        PEc = [base_pe + i + 1 for i in range(NTT)]
        ACc, DVc = [None] * NTT, [None] * NTT
        _a, _d = base_ac, base_dv
        for i in range(NTT):
            if i % 2 == 0:
                _a += 1
                ACc[i] = _a
            else:
                _d += 1
                DVc[i] = _d

        def wf_t(i):
            if i < 4:
                return wfa[:, i * DM:(i + 1) * DM]
            if i < 6:
                return wfb[:, (i - 4) * DM:(i - 4 + 1) * DM]
            return wfc[:, (i - 6) * DM:(i - 6 + 1) * DM]

        def kq_dst(kind, b, c):
            if kind == "k":
                return kT[:, b * SEQ + c * 512: b * SEQ + c * 512 + 512]
            return qT_sl(b, 0, NP, c * 512, 512)

        @blk.tensor
        def _(t):
            for i, (kind, b, c) in enumerate(tiles):
                if i >= 8:
                    j = i - 8
                    if j % 2 == 0:
                        w(t, "ac", ACc[j])
                    else:
                        w(t, "dv", DVc[j])
                bk = bank(i % 8)
                if kind == "k":
                    for lt in range(2):
                        mm = t.matmul(
                            bk,
                            bkT[:, lt * DM + b * NP: lt * DM + (b + 1) * NP],
                            klT[:, lt * SEQ + c * 512: lt * SEQ + (c + 1) * 512],
                            start=(lt == 0), stop=(lt == 1))
                else:
                    for dt in range(NDT):
                        mm = t.matmul(
                            bk,
                            wqT[:, dt * DM + b * NP: dt * DM + (b + 1) * NP],
                            xT_sl(dt, c * 512, 512),
                            start=(dt == 0), stop=(dt == NDT - 1))
                assert inc(mm, "pe") == PEc[i]

        @blk.scalar
        def _(s):
            for i, (kind, b, c) in enumerate(tiles):
                if i % 2:
                    continue
                s.wait_ge(sems["pe"], PEc[i])
                w(s, "ac", cnt["ac"])
                assert inc(s.copy(kq_dst(kind, b, c), bank(i % 8)),
                           "ac") == ACc[i]

        @blk.sync
        def _(sy):
            M["d_wfb"] = dma(sy, "dsy",
                             wfb[:].rearrange("p (t d) -> p t d", d=DM),
                             wf8[:, 4:6, :])
            M["d_wfc"] = dma(sy, "dsy",
                             wfc[:].rearrange("p (t d) -> p t d", d=DM),
                             wf8[:, 6:8, :])

        @blk.vector
        def _(v):
            for i, (kind, b, c) in enumerate(tiles):
                if i % 2 == 0:
                    continue
                v.wait_ge(sems["pe"], PEc[i])
                c2 = dvo(v, lambda kind=kind, b=b, c=c, i=i: v.tensor_copy(
                    kq_dst(kind, b, c), bank(i % 8)))
                assert c2 == DVc[i]
            wdma(v, M["d_wfa"])
            wdma(v, M["d_wfb"])
            wdma(v, M["d_wfc"])
            for i in range(NDT):
                dvo(v, lambda i=i: v.tensor_reduce(
                    FPART[:, i:i + 1], wf_t(i),
                    AX.X, ALU.add, apply_absolute_value=True))
            M["dv_fsum"] = dvo(v, lambda: v.tensor_reduce(
                FPSUM, FPART[:, 0:NDT], AX.X, ALU.add))

    with nc.Block("p1_vwt") as blk:
        base_pe, base_ac, base_dv = cnt["pe"], cnt["ac"], cnt["dv"]

        def wf_t(i):
            if i < 4:
                return wfa[:, i * DM:(i + 1) * DM]
            if i < 6:
                return wfb[:, (i - 4) * DM:(i - 4 + 1) * DM]
            return wfc[:, (i - 6) * DM:(i - 6 + 1) * DM]

        VMM = [base_pe + st + 1 for st in range(NST)]
        PE_D = base_pe + NST + 1
        PE_A = base_pe + NST + 2
        PE_B = base_pe + NST + 3
        WTR = [base_pe + NST + 3 + i + 1 for i in range(NDT)]
        VEV = [base_ac + st + 1 for st in range(NST)]
        WEV = [base_ac + NST + i + 1 for i in range(NDT)]
        DV_STAT = M["dv_fsum"]
        DV_DSC = base_dv + 1
        DV_DPN = DV_DSC + 2
        DV_ABC = DV_DPN + 1
        WTQ = [DV_ABC + 3 * (i + 1) for i in range(NDT)]

        @blk.tensor
        def _(t):
            for st in range(NST):
                if st >= 2:
                    w(t, "ac", VEV[st - 2])
                for oc in range(2):
                    for lt in range(2):
                        mm = t.matmul(bank(2 * (st % 2) + oc),
                                      vlT[:, lt * SEQ + st * NP:
                                          lt * SEQ + (st + 1) * NP],
                                      bvT[:, lt * DM + oc * 512:
                                          lt * DM + oc * 512 + 512],
                                      start=(lt == 0), stop=(lt == 1))
                assert inc(mm, "pe") == VMM[st]
            w(t, "dv", DV_STAT)
            assert inc(t.matmul(bank(6)[0:1, 0:1], FPSUM, ONESC,
                                start=True, stop=True), "pe") == PE_D
            w(t, "dv", DV_DSC)
            assert inc(t.matmul(bank(6)[:, 4:5], onesf[:], stat_sb[:, 0:1],
                                start=True, stop=True), "pe") == PE_A
            assert inc(t.matmul(bank(6)[:, 8:9], onesf[:], alpha_sb[:],
                                start=True, stop=True), "pe") == PE_B
            for i in range(NDT):
                w(t, "dv", WTQ[i])
                if i >= 2:
                    w(t, "ac", WEV[i - 2])
                bb = bankb(4 + (i % 2), 1)
                for j in range(NDT):
                    mm = t.transpose(bb[:, j * NP:(j + 1) * NP],
                                     wttmp[:, DM + j * NP: DM + (j + 1) * NP],
                                     ident[:])
                assert inc(mm, "pe") == WTR[i]

        @blk.vector
        def _(v):
            w(v, "pe", PE_D)
            c = dvo(v, lambda: v.tensor_scalar(stat_sb[:, 0:1],
                                               bank(6)[0:1, 0:1],
                                               0.7 / float(DM * DM), None,
                                               ALU.mult))
            assert c == DV_DSC
            w(v, "pe", PE_A)
            dvo(v, lambda: v.tensor_copy(DELTAP, bank(6)[:, 4:5]))
            c = dvo(v, lambda: v.tensor_scalar(DELTAN, DELTAP, -1.0, None,
                                               ALU.mult))
            assert c == DV_DPN
            w(v, "pe", PE_B)
            c = dvo(v, lambda: v.tensor_copy(ALPHA_V, bank(6)[:, 8:9]))
            assert c == DV_ABC
            for i in range(NDT):
                wst = wf_t(i)
                dvo(v, lambda: v.tensor_scalar(wttmp[:, 0:DM], wst,
                                               DELTAP, ALPHA_V,
                                               ALU.is_gt, ALU.mult))
                dvo(v, lambda: v.tensor_scalar(wttmp2[:], wst,
                                               DELTAN, ALPHA_V,
                                               ALU.is_lt, ALU.mult))
                if i >= 1:
                    w(v, "pe", WTR[i - 1])
                c = dvo(v, lambda: v.tensor_tensor(wttmp[:, DM:2 * DM],
                                                   wttmp[:, 0:DM], wttmp2[:],
                                                   ALU.subtract))
                assert c == WTQ[i]

        @blk.scalar
        def _(s):
            for st in range(NST):
                s.wait_ge(sems["pe"], VMM[st])
                w(s, "ac", cnt["ac"])
                assert inc(s.copy(v_sl(st, 0, DM), bank(2 * (st % 2), 2)),
                           "ac") == VEV[st]
            for i in range(NDT):
                s.wait_ge(sems["pe"], WTR[i])
                w(s, "ac", cnt["ac"])
                bb = bankb(4 + (i % 2), 1)
                assert inc(s.copy(
                    bass.AP(wtT, i * NP, [[NDT * DM, NP], [DM, NDT], [1, NP]]),
                    bass.AP(psb, bb.offset, [[8 * 1024, NP], [NP, NDT], [1, NP]])),
                    "ac") == WEV[i]

    ST = [bank(0, 2), bank(2, 2)]
    PV = bank(4, 2)
    DEN = bank(6, 2)
    base = {s: cnt[s] for s in ("sc0", "sc1", "ex0", "ex1", "pvs", "eps")}
    assert all(v == 0 for v in base.values())

    def sp_decode(sp):
        sqc, p = divmod(sp, NPAIR)
        return sqc, p

    def emit_pv_step(t, g, start, stop, tail_split=False, eps_sp=None):
        sp, skt = divmod(g, NST)
        sqc, p = sp_decode(sp)
        st_of = [v_sl(skt, (2 * p + h) * HD, HD) for h in range(2)]
        for c in range(2):
            if eps_sp is not None:
                w(t, "eps", 2 * eps_sp - 1 + c)
            rh0 = pring(0)[:, (g % 4) * SQW + c * 512:(g % 4) * SQW + c * 512 + 512]
            rh1 = pring(1)[:, (g % 4) * SQW + c * 512:(g % 4) * SQW + c * 512 + 512]
            t.matmul(PV[0:64, c * 512:(c + 1) * 512], st_of[0], rh0,
                     start=start, stop=stop, tile_position=(0, 0))
            t.matmul(PV[64:128, c * 512:(c + 1) * 512], st_of[1], rh1,
                     start=start, stop=stop, tile_position=(0, 64))
            t.matmul(DEN[0:64, c * 512:(c + 1) * 512], ones_den[:], rh0,
                     start=start, stop=stop, tile_position=(0, 0))
            mm = t.matmul(DEN[64:128, c * 512:(c + 1) * 512], ones_den[:], rh1,
                          start=start, stop=stop, tile_position=(0, 64))
            if tail_split and c == 0:
                inc(mm, "pvt")
        inc(mm, "pvs")

    with nc.Block("p2") as blk:
        @blk.tensor
        def _(t):
            for sp in range(NSP):
                sqc, p = sp_decode(sp)
                for skt in range(NST):
                    g = sp * NST + skt
                    w(t, "ex0", g)
                    for c in range(2):
                        col = sqc * SQW + c * 512
                        mm0 = t.matmul(
                            ST[0][:, c * 512:(c + 1) * 512],
                            kT[0:64, p * SEQ + skt * NP: p * SEQ + (skt + 1) * NP],
                            qT_sl(p, 0, 64, col, 512),
                            start=True, stop=True, tile_position=(0, 0))
                    inc(mm0, "sc0")
                    w(t, "ex1", g)
                    for c in range(2):
                        col = sqc * SQW + c * 512
                        mm1 = t.matmul(
                            ST[1][:, c * 512:(c + 1) * 512],
                            kT[64:128, p * SEQ + skt * NP: p * SEQ + (skt + 1) * NP],
                            qT_sl(p, 64, 64, col, 512),
                            start=True, stop=True, tile_position=(64, 0))
                    inc(mm1, "sc1")
                    if skt == 0 and sp > 0:
                        gp_ = sp * NST - 1
                        w(t, "ex0", gp_ + 1)
                        w(t, "ex1", gp_ + 1)
                        emit_pv_step(t, gp_, start=False, stop=True,
                                     tail_split=True)
                    if skt > 0:
                        emit_pv_step(t, g - 1, start=(skt - 1 == 0),
                                     stop=(skt - 1 == NST - 1 and sp == NSP - 1),
                                     eps_sp=(sp if skt == 1 and sp > 0 else None))
                if sp == NSP - 1:
                    g_last = sp * NST + NST - 1
                    w(t, "ex0", g_last + 1)
                    w(t, "ex1", g_last + 1)
                    emit_pv_step(t, g_last, start=False, stop=True,
                                 tail_split=True)

        @blk.scalar
        def _(s):
            for sp in range(NSP):
                for skt in range(NST):
                    g = sp * NST + skt
                    w(s, "sc0", g + 1)
                    if g >= 4 and g % 2 == 0:
                        w(s, "pvs", g - 2)
                    inc(s.activation(
                        pring(0)[:, (g % 4) * SQW:(g % 4 + 1) * SQW],
                        ST[0], AF.Exp, scale=0.125), "ex0")
                    w(s, "sc1", g + 1)
                    inc(s.activation(
                        pring(1)[:, (g % 4) * SQW:(g % 4 + 1) * SQW],
                        ST[1], AF.Exp, scale=0.125), "ex1")

        @blk.vector
        def _(v):
            for sp in range(NSP):
                sqc, p = sp_decode(sp)
                w(v, "pvt", sp + 1)
                w(v, "dv", cnt["dv"])
                inc(v.tensor_copy(pvden_stage[:, 0:512], bank(4, 1)), "dv")
                inc(v.tensor_copy(pvden_stage[:, SQW:SQW + 512], bank(6, 1)),
                    "eps")
                w(v, "pvs", (sp + 1) * NST)
                inc(v.tensor_copy(pvden_stage[:, 512:SQW], bank(5, 1)), "dv")
                inc(v.tensor_copy(pvden_stage[:, SQW + 512:2 * SQW],
                                  bank(7, 1)), "eps")
                w(v, "eps", 2 * (sp + 1))
                dvo(v, lambda: v.reciprocal(rd_bc[:], pvden_stage[:, SQW:]))
                dvo(v, lambda sp=sp, sqc=sqc, p=p: v.tensor_tensor(
                    OT[:, p * SEQ + sqc * SQW: p * SEQ + (sqc + 1) * SQW],
                    pvden_stage[:, 0:SQW], rd_bc[:], ALU.mult))

    def lnv3(st, k):
        return vec[:, 16 + (st % 3) * 8 + k: 16 + (st % 3) * 8 + k + 1]

    with nc.Block("p3") as blk:
        P3VAR = [None] * NST
        MIDC = [None] * NST
        SQC = [None] * NST
        REC = [None] * NST
        TEVC3 = [None] * NST
        LNRC = [None] * NST
        WOC = [None] * NST
        TRC3 = [None] * NST
        basedv, baseac = cnt["dv"], cnt["ac"]
        _ac = baseac
        _orda = []
        for i in range(NST):
            if i >= 3:
                _orda.append(("cs", i - 3))
            _orda.append(("sq", i))
            if i >= 1:
                _orda.append(("le", i - 1))
            if i >= 4:
                _orda.append(("tv", i - 4))
        _orda += [("le", NST - 1), ("cs", NST - 3), ("cs", NST - 2),
                  ("cs", NST - 1), ("tv", NST - 4), ("tv", NST - 3),
                  ("tv", NST - 2), ("tv", NST - 1)]
        POSTC = [None] * NST
        _lnbf = 0
        for kind, j in _orda:
            if kind == "sq":
                _ac += 1
                SQC[j] = _ac
            elif kind == "le":
                _ac += 2
                REC[j] = _ac
            elif kind == "cs":
                _lnbf += 1
                POSTC[j] = _lnbf
            else:
                _ac += 2
                TEVC3[j] = _ac
        _dv = basedv
        _lnr = 0
        _ordd = []
        for i in range(NST):
            _ordd.append(("pre", i))
            if i >= 1:
                _ordd.append(("mid", i - 1))
            if i >= 2:
                _ordd.append(("post", i - 2))
        _ordd += [("mid", NST - 1), ("post", NST - 2), ("post", NST - 1)]
        PSTC = [None] * NST
        for kind, j in _ordd:
            if kind == "pre":
                _lnr += 1
                LNRC[j] = _lnr
            elif kind == "mid":
                _dv += 4
                MIDC[j] = _dv
            else:
                _dv += 3
                PSTC[j] = _dv
        _wo = 0
        _tr = 0
        _ordp = []
        for i in range(NST):
            _ordp.append(("wo", i))
            if i >= 3:
                _ordp.append(("tr", i - 3))
        _ordp += [("tr", NST - 3), ("tr", NST - 2), ("tr", NST - 1)]
        for kind, j in _ordp:
            if kind == "wo":
                _wo += 1
                WOC[j] = _wo
            else:
                _tr += 1
                TRC3[j] = _tr

        def p3_pair(st):
            return bank(2 * (st % 3), 2)

        def rsl(st):
            return rtile[:, 0:DM] if False else None

        @blk.sync
        def _(sy):
            M["d_xr_t"] = []
            for st in range(NST):
                if st >= 2:
                    sy.wait_ge(sems["lnr"], st - 1)
                M["d_xr_t"].append(dma(
                    sy, "dsy",
                    xres[:, (st % 2) * DM:(st % 2) * DM + DM]
                    .rearrange("p (o d) -> p o d", o=1),
                    x_h.ap().rearrange("(t p) d -> p t d", p=NP)[:, st:st + 1, :]))

        @blk.tensor
        def _(t):
            def emit_wo(st):
                if st >= 3:
                    w(t, "lnr", LNRC[st - 3])
                for oc in range(2):
                    for p in range(NPAIR):
                        mm = t.matmul(
                            p3_pair(st)[:, oc * 512:(oc + 1) * 512],
                            OT[:, p * SEQ + st * NP: p * SEQ + (st + 1) * NP],
                            woT[:, p * DM + oc * 512: p * DM + oc * 512 + 512],
                            start=(p == 0), stop=(p == NPAIR - 1))
                assert inc(mm, "womm") == WOC[st]

            def emit_tr(st):
                w(t, "lnbf", POSTC[st])
                if st >= 2:
                    w(t, "ac", TEVC3[st - 2])
                bb = bankb(6 + (st % 2), 1)
                for j in range(NDT):
                    mm = t.transpose(
                        bb[:, j * NP:(j + 1) * NP],
                        x1bfw[:, (st % 2) * DM + j * NP:
                              (st % 2) * DM + (j + 1) * NP],
                        ident[:])
                assert inc(mm, "x1tr") == TRC3[st]

            for i in range(NST):
                emit_wo(i)
                if i >= 3:
                    emit_tr(i - 3)
            for j in (NST - 3, NST - 2, NST - 1):
                emit_tr(j)

        @blk.vector
        def _(v):
            def pre(st):
                w(v, "womm", WOC[st])
                wdma(v, M["d_xr_t"][st])
                if st >= 3:
                    w(v, "ac", SQC[st - 3])
                r = rtile[:, (st % 3) * DM:(st % 3) * DM + DM]
                w(v, "dv", cnt["dv"])
                assert inc(v.scalar_tensor_tensor(
                    r, p3_pair(st), 1.0,
                    xres[:, (st % 2) * DM:(st % 2) * DM + DM],
                    ALU.mult, ALU.add, accum_out=lnv3(st, 0)),
                    "lnr") == LNRC[st]
                w(v, "lnr", LNRC[st])

            def mid(st):
                v.wait_ge(sems["ac"], SQC[st])
                dvo(v, lambda: v.tensor_scalar(lnv3(st, 2), lnv3(st, 0),
                                               1.0 / DM, None, ALU.mult))
                dvo(v, lambda: v.tensor_scalar(lnv3(st, 3), lnv3(st, 1),
                                               1.0 / DM, EPS, ALU.mult, ALU.add))
                dvo(v, lambda: v.tensor_tensor(lnv3(st, 4), lnv3(st, 2),
                                               lnv3(st, 2), ALU.mult))
                c = dvo(v, lambda: v.tensor_tensor(lnv3(st, 3), lnv3(st, 3),
                                                   lnv3(st, 4), ALU.subtract))
                assert c == MIDC[st]
                P3VAR[st] = c

            def post(st):
                v.wait_ge(sems["ac"], REC[st])
                r = rtile[:, (st % 3) * DM:(st % 3) * DM + DM]
                dvo(v, lambda: v.tensor_tensor(lnv3(st, 6), lnv3(st, 2),
                                               lnv3(st, 5), ALU.mult))
                dvo(v, lambda: v.tensor_scalar(lnv3(st, 6), lnv3(st, 6),
                                               -1.0, None, ALU.mult))
                c = dvo(v, lambda: v.tensor_scalar(x1_t(st), r, lnv3(st, 5),
                                                   lnv3(st, 6), ALU.mult,
                                                   ALU.add))
                assert c == PSTC[st]

            for i in range(NST):
                pre(i)
                if i >= 1:
                    mid(i - 1)
                if i >= 2:
                    post(i - 2)
            mid(NST - 1)
            post(NST - 2)
            post(NST - 1)

        @blk.scalar
        def _(s):
            sqd = sqs[:, 0:DM]

            def sq(st):
                s.wait_ge(sems["lnr"], LNRC[st])
                if st >= 3:
                    s.wait_ge(sems["dv"], MIDC[st - 3])
                r = rtile[:, (st % 3) * DM:(st % 3) * DM + DM]
                w(s, "ac", cnt["ac"])
                assert inc(s.activation(sqd, r, AF.Square,
                                        accum_out=lnv3(st, 1)), "ac") == SQC[st]

            def lnexp(st):
                s.wait_ge(sems["dv"], MIDC[st])
                w(s, "ac", cnt["ac"])
                inc(s.activation(lnv3(st, 5), lnv3(st, 3), AF.Ln), "ac")
                w(s, "ac", cnt["ac"])
                assert inc(s.activation(lnv3(st, 5), lnv3(st, 5), AF.Exp,
                                        scale=-0.5), "ac") == REC[st]

            def cast(st):
                s.wait_ge(sems["dv"], PSTC[st])
                if st >= 2:
                    s.wait_ge(sems["x1tr"], TRC3[st - 2])
                w(s, "ac", cnt["ac"])
                assert inc(s.copy(
                    x1bfw[:, (st % 2) * DM:(st % 2) * DM + DM], x1_t(st)),
                    "lnbf") == POSTC[st]

            def tev(st):
                s.wait_ge(sems["x1tr"], TRC3[st])
                bb = bankb(6 + (st % 2), 1)
                col = st * NP
                w(s, "ac", cnt["ac"])
                inc(s.copy(
                    bass.AP(x1Ta, col, [[4 * SEQ, NP], [SEQ, 4], [1, NP]]),
                    bass.AP(psb, bb.offset, [[8 * 1024, NP], [NP, 4], [1, NP]])),
                    "ac")
                w(s, "ac", cnt["ac"])
                assert inc(s.copy(
                    bass.AP(x1Tb, col, [[4 * SEQ, NP], [SEQ, 4], [1, NP]]),
                    bass.AP(psb, bb.offset + 4 * NP,
                            [[8 * 1024, NP], [NP, 4], [1, NP]])),
                    "ac") == TEVC3[st]

            for i in range(NST):
                if i >= 3:
                    cast(i - 3)
                sq(i)
                if i >= 1:
                    lnexp(i - 1)
                if i >= 4:
                    tev(i - 4)
            lnexp(NST - 1)
            for j in (NST - 3, NST - 2, NST - 1):
                cast(j)
            for j in (NST - 4, NST - 3, NST - 2, NST - 1):
                tev(j)

    with nc.Block("p4") as blk:
        P4VAR = [None] * NST
        P4PST = [None] * NST
        MID4 = [None] * NST
        SQ4 = [None] * NST
        RE4 = [None] * NST
        FFC = [None] * NST
        basedv4, baseac4 = cnt["dv"], cnt["ac"]
        _ac = baseac4
        for i in range(NST):
            _ac += 1
            SQ4[i] = _ac
            if i >= 1:
                _ac += 2
                RE4[i - 1] = _ac
        RE4[NST - 1] = _ac + 2
        _rr = dma_rr["dsy"]
        _pc = {i: cnt[f"dsy{i}"] for i in range(NDMASEM)}
        M["d_out_t"] = []
        for st in range(NST):
            i = (_rr + st) % NDMASEM
            _pc[i] += 16
            M["d_out_t"].append((f"dsy{i}", _pc[i]))

        r2ring = place("r2ring", [NP, 3 * DM], F32, 150 * KB)
        sqd4 = place("sqd4", [NP, DM], F32, 162 * KB)
        lnv4t = lnv3

        def r2sl(st):
            return r2ring[:, (st % 3) * DM:(st % 3) * DM + DM]

        @blk.tensor
        def _(t):
            wdma(t, M["d_blo"])
            for st in range(NST):
                if st >= 4:
                    w(t, "ffnr", st - 3)
                for oc in range(2):
                    t.matmul(bank(2 * (st % 4) + oc), ones2[:],
                             bias_hl[:, oc * 512:(oc + 1) * 512],
                             start=True, stop=False)
                    for dt in range(NDT):
                        xs = x1Ta if dt < 4 else x1Tb
                        mm = t.matmul(
                            bank(2 * (st % 4) + oc),
                            xs[:, (dt % 4) * SEQ + st * NP:
                               (dt % 4) * SEQ + (st + 1) * NP],
                            wtT[:, dt * DM + oc * 512: dt * DM + oc * 512 + 512],
                            start=False, stop=(dt == NDT - 1))
                inc(mm, "ffnm")

        @blk.vector
        def _(v):
            def pre(st):
                w(v, "ffnm", st + 1)
                if st >= 3:
                    w(v, "ac", SQ4[st - 3])
                    w(v, "ln2ap", st - 2)
                r2 = r2sl(st)
                w(v, "dv", cnt["dv"])
                inc(v.scalar_tensor_tensor(
                    r2, bank(2 * (st % 4), 2), 1.0,
                    x1_t(st), ALU.mult, ALU.add, accum_out=lnv4t(st, 0)),
                    "ffnr")
                w(v, "ffnr", st + 1)

            def mid(st):
                v.wait_ge(sems["ac"], SQ4[st])
                dvo(v, lambda: v.tensor_scalar(lnv4t(st, 2), lnv4t(st, 0),
                                               1.0 / DM, None, ALU.mult))
                dvo(v, lambda: v.tensor_scalar(lnv4t(st, 3), lnv4t(st, 1),
                                               1.0 / DM, EPS, ALU.mult, ALU.add))
                dvo(v, lambda: v.tensor_scalar(lnv4t(st, 4), lnv4t(st, 2),
                                               lnv4t(st, 2), None, ALU.mult))
                P4VAR[st] = dvo(v, lambda: v.tensor_tensor(
                    lnv4t(st, 3), lnv4t(st, 3), lnv4t(st, 4), ALU.subtract))
                MID4[st] = P4VAR[st]

            def post(st):
                v.wait_ge(sems["ac"], RE4[st])
                r2 = r2sl(st)
                dvo(v, lambda: v.tensor_tensor(lnv4t(st, 6), lnv4t(st, 2),
                                               lnv4t(st, 5), ALU.mult))
                dvo(v, lambda: v.tensor_scalar(lnv4t(st, 6), lnv4t(st, 6),
                                               -1.0, None, ALU.mult))
                if st >= 2:
                    wdma(v, M["d_out_t"][st - 2])
                w(v, "dv", cnt["dv"])
                inc(v.tensor_scalar(
                    ostage[:, (st % 2) * DM:(st % 2) * DM + DM],
                    r2, lnv4t(st, 5), lnv4t(st, 6), ALU.mult, ALU.add), "ln2ap")

            for i in range(NST):
                pre(i)
                if i >= 1:
                    mid(i - 1)
                if i >= 2:
                    post(i - 2)
            mid(NST - 1)
            post(NST - 2)
            post(NST - 1)

        @blk.scalar
        def _(s):
            def sq(st):
                s.wait_ge(sems["ffnr"], st + 1)
                if st >= 3:
                    s.wait_ge(sems["dv"], MID4[st - 3])
                w(s, "ac", cnt["ac"])
                assert inc(s.activation(sqd4[:], r2sl(st), AF.Square,
                                        accum_out=lnv4t(st, 1)), "ac") == SQ4[st]

            def lnexp(st):
                s.wait_ge(sems["dv"], P4VAR[st])
                w(s, "ac", cnt["ac"])
                inc(s.activation(lnv4t(st, 5), lnv4t(st, 3), AF.Ln), "ac")
                w(s, "ac", cnt["ac"])
                assert inc(s.activation(lnv4t(st, 5), lnv4t(st, 5), AF.Exp,
                                        scale=-0.5), "ac") == RE4[st]

            for i in range(NST):
                sq(i)
                if i >= 1:
                    lnexp(i - 1)
            lnexp(NST - 1)

        @blk.sync
        def _(sy):
            for st in range(NST):
                sy.wait_ge(sems["ln2ap"], st + 1)
                tok = dma(
                    sy, "dsy",
                    out_h.ap().rearrange("(t p) d -> p t d", p=NP)[:, st:st + 1, :],
                    ostage[:, (st % 2) * DM:(st % 2) * DM + DM]
                    .rearrange("p (o d) -> p o d", o=1))
                assert tok == M["d_out_t"][st], (tok, M["d_out_t"][st])
            for i in range(NDMASEM):
                sy.wait_ge(sems[f"dsy{i}"], cnt[f"dsy{i}"])

    return nc


_PROG_CACHE = {}


def _get_program():
    if "nc" not in _PROG_CACHE:
        _PROG_CACHE["nc"] = build_program()
    return _PROG_CACHE["nc"]


def _rot_matrices():
    inv = 1.0 / (10000.0 ** (np.arange(0, HD, 2, dtype=np.float64)[: DROPE // 2] / HD))
    R = np.zeros((NP, NPAIR * NP), dtype=np.float64)
    for pb in range(NPAIR):
        blk = np.eye(NP, dtype=np.float64)
        for j in range(2):
            h = 2 * pb + j
            for k in range(DROPE // 2):
                c, s = np.cos(h * inv[k]), np.sin(h * inv[k])
                e, o = j * HD + 2 * k, j * HD + 2 * k + 1
                blk[e, e] = c
                blk[o, e] = -s
                blk[e, o] = s
                blk[o, o] = c
        R[:, pb * NP:(pb + 1) * NP] = blk
    hi = R.astype(ml_dtypes.bfloat16)
    lo = (R - hi.astype(np.float64)).astype(ml_dtypes.bfloat16)
    return np.ascontiguousarray(hi), np.ascontiguousarray(lo)


def kernel(L0_attn_x, L0_attn_wq, L0_attn_wk_a, L0_attn_wk_b, L0_attn_wv_a,
           L0_attn_wv_b, L0_attn_wo, L0_ffn_weight, L0_ffn_alpha, L0_ffn_bias):
    nc = _get_program()
    rhi, rlo = _rot_matrices()
    ident = np.eye(NP, dtype=ml_dtypes.bfloat16)
    common = {
        "wq": np.ascontiguousarray(L0_attn_wq, dtype=np.float32),
        "wka": np.ascontiguousarray(L0_attn_wk_a, dtype=np.float32),
        "wkb": np.ascontiguousarray(L0_attn_wk_b, dtype=np.float32),
        "wva": np.ascontiguousarray(L0_attn_wv_a, dtype=np.float32),
        "wvb": np.ascontiguousarray(L0_attn_wv_b, dtype=np.float32),
        "wo": np.ascontiguousarray(L0_attn_wo, dtype=np.float32),
        "wf": np.ascontiguousarray(L0_ffn_weight, dtype=np.float32),
        "alpha": np.asarray(L0_ffn_alpha, dtype=np.float32).reshape(1, 1),
        "fbias": np.asarray(L0_ffn_bias, dtype=np.float32).reshape(1, DM),
        "identi": ident,
        "rhi": rhi, "rlo": rlo,
    }
    x = np.asarray(L0_attn_x, dtype=np.float32)
    in_maps = [dict(common, x=np.ascontiguousarray(x[c])) for c in range(8)]
    res = run_bass_kernel_spmd(nc, in_maps, core_ids=list(range(8)))
    out = np.stack([np.asarray(res.results[c]["out"]) for c in range(8)], axis=0)
    return out.astype(np.float32)


# version 33
# speedup vs baseline: 1.0118x; 1.0118x over previous
import numpy as np
import ml_dtypes
import concourse.bass as bass
import concourse.mybir as mybir
from concourse.bass_utils import run_bass_kernel_spmd

F32 = mybir.dt.float32
BF16 = mybir.dt.bfloat16
AF = mybir.ActivationFunctionType
ALU = mybir.AluOpType
AX = mybir.AxisListType

SEQ, DM, NH, HD, DROPE, LAT = 2048, 1024, 16, 64, 16, 256
NP = 128
NST = SEQ // NP
NDT = DM // NP
NPAIR = NH // 2
SQW = 1024
NSQC = SEQ // SQW
NSP = NSQC * NPAIR
EPS = 1e-5
KB = 1024


def build_program():
    nc = bass.Bass("TRN2", target_bir_lowering=False)

    x_h = nc.dram_tensor("x", [SEQ, DM], F32, kind="ExternalInput")
    wq_h = nc.dram_tensor("wq", [DM, DM], F32, kind="ExternalInput")
    wka_h = nc.dram_tensor("wka", [LAT, DM], F32, kind="ExternalInput")
    wkb_h = nc.dram_tensor("wkb", [DM, LAT], F32, kind="ExternalInput")
    wva_h = nc.dram_tensor("wva", [LAT, DM], F32, kind="ExternalInput")
    wvb_h = nc.dram_tensor("wvb", [DM, LAT], F32, kind="ExternalInput")
    wo_h = nc.dram_tensor("wo", [DM, DM], F32, kind="ExternalInput")
    wf_h = nc.dram_tensor("wf", [DM, DM], F32, kind="ExternalInput")
    alpha_h = nc.dram_tensor("alpha", [1, 1], F32, kind="ExternalInput")
    bias_h = nc.dram_tensor("fbias", [1, DM], F32, kind="ExternalInput")
    ident_h = nc.dram_tensor("identi", [NP, NP], BF16, kind="ExternalInput")
    rhi_h = nc.dram_tensor("rhi", [NP, NPAIR * NP], BF16, kind="ExternalInput")
    rlo_h = nc.dram_tensor("rlo", [NP, NPAIR * NP], BF16, kind="ExternalInput")
    out_h = nc.dram_tensor("out", [SEQ, DM], F32, kind="ExternalOutput")

    ARENA = 206 * KB
    arena = nc.alloc_sbuf_tensor("arena", [NP, ARENA], mybir.dt.uint8)
    arena_base = nc.lookup_mloc(arena).addr

    def place(name, shape, dtype, off):
        per_part = int(np.prod(shape[1:])) * mybir.dt.size(dtype)
        assert off % 32 == 0 and off + per_part <= ARENA, (name, off, per_part)
        h = nc.virtual_sbuf_tensor(name, list(shape), dtype)
        nc._tensor(h.name, list(h.shape), h.dtype, type="SB")
        mloc = nc.lookup_mloc(h)
        mloc.allocated = not nc.target_bir_lowering
        mloc.addr = arena_base + off
        return h

    ident = place("identb", [NP, NP], BF16, 0)
    ones_den = place("ones_den", [NP, HD], BF16, 256)
    onesf = place("onesf", [1, NP], F32, 384)
    vec = place("vec", [NP, 64], F32, 1 * KB)
    bias_row = place("bias_row", [1, DM], F32, 1 * KB + 256)
    alpha_sb = place("alpha_sb", [1, 1], F32, 5 * KB + 256)
    stat_sb = place("stat_sb", [1, 8], F32, 5 * KB + 288)
    rd_bc = place("rd_bc", [NP, SQW], F32, 5 * KB + 512)
    x1bfw = place("x1bfw", [NP, 2 * DM], BF16, 5 * KB + 512)
    ones2 = place("ones2", [2, NP], BF16, 9 * KB + 512)
    bias_hl = place("bias_hl", [2, DM], BF16, 10 * KB)
    bias_lo = place("bias_lo", [1, DM], BF16, 12 * KB)
    woT = place("woT", [NP, NDT * DM], BF16, 14 * KB)
    wqstage = place("wqstage", [NP, NDT * DM], BF16, 78 * KB)
    xTb = place("xTb", [NP, 4 * SEQ], BF16, 30 * KB)
    wtT = place("wtT", [NP, NDT * DM], BF16, 30 * KB)
    wfa = place("wfa", [NP, 4 * DM], F32, 142 * KB)
    wfb = place("wfb", [NP, 2 * DM], F32, 174 * KB)
    wfc = place("wfc", [NP, 2 * DM], F32, 110 * KB)
    akT = place("akT", [NP, NDT * 2 * NP], BF16, 46 * KB)
    avT = place("avT", [NP, NDT * 2 * NP], BF16, 50 * KB)
    bkT = place("bkT", [NP, 2 * DM], BF16, 54 * KB)
    bvT = place("bvT", [NP, 2 * DM], BF16, 58 * KB)
    qTb = place("qTb", [NP, 2 * SEQ], BF16, 46 * KB)
    pring1 = place("pring1", [NP, 4 * SQW], BF16, 54 * KB)
    x1Ta = place("x1Ta", [NP, 4 * SEQ], BF16, 46 * KB)
    wostage = place("wostage", [NP, NDT * DM], BF16, 62 * KB)
    wqT = place("wqT", [NP, NDT * DM], BF16, 62 * KB)
    vfa = place("vfa", [NP, 8 * DM], BF16, 62 * KB)
    x1Tb = place("x1Tb", [NP, 4 * SEQ], BF16, 62 * KB)
    kT = place("kT", [NP, NDT * SEQ], BF16, 78 * KB)
    x1a = place("x1a", [NP, 8 * DM], F32, 78 * KB)
    wkbstage = place("wkbstage", [NP, 2 * DM], BF16, 94 * KB)
    wvbstage = place("wvbstage", [NP, 2 * DM], BF16, 98 * KB)
    kqstage = place("kqstage", [NP, 4 * DM], BF16, 110 * KB)
    pvden_stage = place("pvden_stage", [NP, 2 * SQW], F32, 110 * KB)
    wttmp = place("wttmp", [NP, 2 * DM], BF16, 198 * KB)
    wttmp2 = place("wttmp2", [NP, DM], BF16, 202 * KB)
    qTa = place("qTa", [NP, 6 * SEQ], BF16, 118 * KB)
    x1b = place("x1b", [NP, 8 * DM], F32, 110 * KB)
    xbf = place("xbf", [NP, NST * DM], BF16, 142 * KB)
    rhi_sb = place("rhi_sb", [NP, NPAIR * NP], BF16, 118 * KB)
    rlo_sb = place("rlo_sb", [NP, NPAIR * NP], BF16, 120 * KB)
    klT = place("klT", [NP, 2 * SEQ], BF16, 158 * KB)
    vlT = place("vlT", [NP, 2 * SEQ], BF16, 166 * KB)
    OT = place("OT", [NP, NPAIR * SEQ], BF16, 142 * KB)
    ttile = place("ttile", [NP, 2 * DM], F32, 142 * KB)
    r2tile = place("r2tile", [NP, 2 * DM], F32, 150 * KB)
    pring0 = place("pring0", [NP, 4 * SQW], BF16, 174 * KB)
    xTa = place("xTa", [NP, 4 * SEQ], BF16, 182 * KB)
    vfb = place("vfb", [NP, 8 * DM], BF16, 182 * KB)
    xres = place("xres", [NP, 2 * DM], F32, 174 * KB)
    rtile = place("rtile", [NP, 3 * DM], F32, 182 * KB)
    sqs = place("sqs", [NP, DM], F32, 194 * KB)
    ostage = place("ostage", [NP, 2 * DM], F32, 198 * KB)

    def xT_sl(dt, col, width):
        t = xTa if dt < 4 else xTb
        return t[:, (dt % 4) * SEQ + col: (dt % 4) * SEQ + col + width]

    def qT_sl(p, p0, nparts, col, width):
        t, i = (qTa, p) if p < 6 else (qTb, p - 6)
        return t[p0:p0 + nparts, i * SEQ + col: i * SEQ + col + width]

    def v_sl(st, col, width):
        t = vfa if st < 8 else vfb
        return t[:, (st % 8) * DM + col: (st % 8) * DM + col + width]

    def x1_t(st):
        t = x1a if st < 8 else x1b
        return t[:, (st % 8) * DM: (st % 8) * DM + DM]

    def pring(h):
        return pring0 if h == 0 else pring1

    ps = nc.psum_tensor("ps", [NP, 8 * 512], F32).__enter__()
    psb = nc.virtual_psum_tensor("psb", [NP, 8 * 1024], BF16)
    nc._tensor(psb.name, list(psb.shape), psb.dtype, type="PSUM")
    _pm = nc.lookup_mloc(psb)
    _pm.allocated = not nc.target_bir_lowering
    _pm.addr = nc.lookup_mloc(ps).addr

    def bank(i, n=1):
        return ps[:, i * 512: (i + n) * 512]

    def bankb(i, n=1):
        return psb[:, i * 1024: (i + n) * 1024]

    cnt = {}
    sems = {}
    for n in ("pe", "dv", "ac", "gp",
              "sc0", "sc1", "ex0", "ex1", "pvs", "eps", "pvt",
              "womm", "lnr", "lnbf", "x1tr", "x1ev",
              "ffnm", "ffne", "ffnr", "ln2ap", "gpt"):
        sems[n] = nc.semaphore(n).__enter__()
        cnt[n] = 0
    NDMASEM = 8
    for q in ("dgp", "dsy"):
        for i in range(NDMASEM):
            n = f"{q}{i}"
            sems[n] = nc.semaphore(n).__enter__()
            cnt[n] = 0
    dma_rr = {"dgp": 0, "dsy": 0}

    def inc(instr, s, n=1):
        instr.then_inc(sems[s], n)
        cnt[s] += n
        return cnt[s]

    def w(eng, s, val):
        if val > 0:
            eng.wait_ge(sems[s], val)

    def dma(q, pool, dst, src, transpose=False):
        i = dma_rr[pool] % NDMASEM
        dma_rr[pool] += 1
        s = f"{pool}{i}"
        inc(q.dma_start(dst, src, transpose=transpose), s, 16)
        return (s, cnt[s])

    def wdma(eng, tok):
        eng.wait_ge(sems[tok[0]], tok[1])

    def dvo(v, emit):
        w(v, "dv", cnt["dv"])
        return inc(emit(), "dv")

    def aco(a, emit):
        w(a, "ac", cnt["ac"])
        return inc(emit(), "ac")

    def gpo(g, emit):
        w(g, "gp", cnt["gp"])
        return inc(emit(), "gp")

    def vs(i):
        return vec[:, i:i + 1]

    ALPHA_V, DELTAP, DELTAN, ONESC = vs(0), vs(1), vs(2), vs(3)
    FPART = vec[:, 4:12]
    FPSUM = vs(12)

    def lnv(st, k):
        i = 16 + (st % 2) * 8 + k
        return vec[:, i:i + 1]

    M = {}

    def transpose_job(name, stage, stage_base, n_ot, d_ext, dst_sl,
                      wait_tok=None):
        n_dt = d_ext // NP
        base_pe, base_ac = cnt["pe"], cnt["ac"]

        with nc.Block(name) as blk:
            @blk.tensor
            def _(t):
                if wait_tok is not None:
                    wdma(t, wait_tok)
                for j in range(n_dt):
                    if j >= 2:
                        w(t, "ac", base_ac + (j - 2) + 1)
                    bb = bankb(4 + 2 * (j % 2), 1)
                    for i in range(n_ot):
                        c0 = stage_base + i * d_ext + j * NP
                        mm = t.transpose(bb[:, i * NP:(i + 1) * NP],
                                         stage[:, c0:c0 + NP], ident[:])
                    inc(mm, "pe")

            @blk.scalar
            def _(s):
                for j in range(n_dt):
                    w(s, "pe", base_pe + j + 1)
                    w(s, "ac", cnt["ac"])
                    inc(s.copy(dst_sl(j),
                               bankb(4 + 2 * (j % 2), 1)[:, 0:n_ot * NP]), "ac")

    with nc.Block("p00", no_gpsimd_drain=True) as blk:
        @blk.scalar
        def _(s):
            from concourse.hw_specs import get_activation_tables
            names = list(get_activation_tables(nc.m.arch).keys())
            s.add_instruction(mybir.InstLoadActFuncSet(
                name=nc.get_next_instruction_name(),
                act_func_set_id=names.index("natural_log_exp_and_others"),
                ins=[], outs=[]))

        @blk.vector
        def _(v):
            dvo(v, lambda: v.memset(ones_den[:], 1.0))
            dvo(v, lambda: v.memset(onesf[:], 1.0))
            dvo(v, lambda: v.memset(ONESC, 1.0))
            dvo(v, lambda: v.memset(ones2[:], 1.0))
            M["dv_const"] = cnt["dv"]

        @blk.gpsimd
        def _(g):
            M["d_x"] = [
                dma(g, "dgp",
                    xbf[:, h * 8 * DM:(h + 1) * 8 * DM]
                    .rearrange("p (t d) -> p t d", d=DM),
                    x_h.ap().rearrange("(t p) d -> p t d", p=NP)[:, h * 8:
                                                                (h + 1) * 8, :])
                for h in range(2)]
            M["d_wka"] = dma(g, "dgp",
                             kqstage[:, 0:2 * DM].rearrange(
                                 "p (t d) -> p t d", d=DM),
                             wka_h.ap().rearrange("(t p) d -> p t d", p=NP))
            M["d_wva"] = dma(g, "dgp",
                             kqstage[:, 2 * DM:4 * DM].rearrange(
                                 "p (t d) -> p t d", d=DM),
                             wva_h.ap().rearrange("(t p) d -> p t d", p=NP))
            M["d_wkb"] = dma(g, "dgp",
                             wkbstage[:].rearrange("p (t d) -> p t d", d=LAT),
                             wkb_h.ap().rearrange("(t p) d -> p t d", p=NP))
            M["d_wvb"] = dma(g, "dgp",
                             wvbstage[:].rearrange("p (t d) -> p t d", d=LAT),
                             wvb_h.ap().rearrange("(t p) d -> p t d", p=NP))
            M["d_wo"] = dma(g, "dgp",
                            wostage[:].rearrange("p (t d) -> p t d", d=DM),
                            wo_h.ap().rearrange("(t p) d -> p t d", p=NP))
            M["d_wq"] = dma(g, "dgp",
                            wqstage[:].rearrange("p (t d) -> p t d", d=DM),
                            wq_h.ap().rearrange("(t p) d -> p t d", p=NP))

        @blk.sync
        def _(sy):
            t1 = dma(sy, "dsy", ident[:], ident_h.ap())
            t2 = dma(sy, "dsy", alpha_sb[:], alpha_h.ap())
            t3 = dma(sy, "dsy", bias_row[:], bias_h.ap())
            M["d_small"] = [t1, t2, t3]
            M["d_R"] = [dma(sy, "dsy", rhi_sb[:], rhi_h.ap()),
                        dma(sy, "dsy", rlo_sb[:], rlo_h.ap())]

    with nc.Block("p1_xT") as blk:
        base_pe, base_dv = cnt["pe"], cnt["dv"]

        @blk.tensor
        def _(t):
            for h in range(2):
                wdma(t, M["d_x"][h])
                if h == 0:
                    for tok in M["d_small"]:
                        wdma(t, tok)
                for dt in range(NDT):
                    if dt >= 2:
                        w(t, "dv", base_dv + h * NDT + (dt - 2) + 1)
                    bb = bankb(4 + 2 * (dt % 2) + h, 1)
                    for j in range(8):
                        st = h * 8 + j
                        mm = t.transpose(
                            bb[:, j * NP:(j + 1) * NP],
                            xbf[:, st * DM + dt * NP: st * DM + (dt + 1) * NP],
                            ident[:])
                    inc(mm, "pe")

        @blk.vector
        def _(v):
            for h in range(2):
                for dt in range(NDT):
                    w(v, "pe", base_pe + h * NDT + dt + 1)
                    dvo(v, lambda h=h, dt=dt: v.tensor_copy(
                        xT_sl(dt, h * 1024, 1024),
                        bankb(4 + 2 * (dt % 2) + h, 1)))

    wf8 = wf_h.ap().rearrange("(t p) d -> p t d", p=NP)
    with nc.Block("p1_tab") as blk:
        @blk.sync
        def _(sy):
            M["d_wfa"] = dma(sy, "dsy",
                             wfa[:].rearrange("p (t d) -> p t d", d=DM),
                             wf8[:, 0:4, :])

        @blk.vector
        def _(v):
            wdma(v, M["d_small"][2])
            dvo(v, lambda: v.tensor_copy(bias_hl[0:1, :], bias_row[:]))
            dvo(v, lambda: v.tensor_tensor(bias_lo[:], bias_row[:],
                                           bias_hl[0:1, :], ALU.subtract))
            M["dv_blo"] = cnt["dv"]

        @blk.gpsimd
        def _(g):
            w(g, "dv", M["dv_blo"])
            M["d_blo"] = dma(g, "dgp", bias_hl[1:2, :], bias_lo[:])

    with nc.Block("p1_jobs") as blk:
        P0, A0, D0 = cnt["pe"], cnt["ac"], cnt["dv"]
        pe_off, ac_off = {}, {}
        _o = P0
        for nm, n in (("akT", 8), ("avT", 8), ("klT", 8), ("vlT", 8),
                      ("rkf", 8), ("bkT", 2), ("bvT", 2), ("woT", 8),
                      ("rqf", 16), ("wqT", 8)):
            pe_off[nm] = _o
            _o += n
        _o = A0
        for nm, n in (("akT", 8), ("avT", 8), ("rkf", 8), ("bkT", 2),
                      ("bvT", 2), ("woT", 8), ("rqf", 16), ("wqT", 8)):
            ac_off[nm] = _o
            _o += n
        dv_off = {"klT": D0, "vlT": D0 + 8}

        TJOBS = {
            "akT": (kqstage, 0, 2, DM,
                    lambda j: akT[:, j * 2 * NP:(j + 1) * 2 * NP]),
            "avT": (kqstage, 2 * DM, 2, DM,
                    lambda j: avT[:, j * 2 * NP:(j + 1) * 2 * NP]),
            "bkT": (wkbstage, 0, NDT, LAT,
                    lambda j: bkT[:, j * DM:(j + 1) * DM]),
            "bvT": (wvbstage, 0, NDT, LAT,
                    lambda j: bvT[:, j * DM:(j + 1) * DM]),
            "woT": (wostage, 0, NDT, DM,
                    lambda j: woT[:, j * DM:(j + 1) * DM]),
            "wqT": (wqstage, 0, NDT, DM,
                    lambda j: wqT[:, j * DM:(j + 1) * DM]),
        }
        LJOBS = {"klT": (akT, klT), "vlT": (avT, vlT)}
        RJOBS = {"rkf": (wkbstage, LAT), "rqf": (wqstage, DM)}

        @blk.tensor
        def _(t):
            def tjob(nm, start_ac, toks):
                stage, base, n_ot, d_ext, _sl = TJOBS[nm]
                for tok in toks:
                    wdma(t, tok)
                w(t, "ac", start_ac)
                for j in range(d_ext // NP):
                    if j >= 2:
                        w(t, "ac", ac_off[nm] + (j - 2) + 1)
                    bb = bankb(4 + 2 * (j % 2), 1)
                    for i in range(n_ot):
                        c0 = base + i * d_ext + j * NP
                        mm = t.transpose(bb[:, i * NP:(i + 1) * NP],
                                         stage[:, c0:c0 + NP], ident[:])
                    inc(mm, "pe")

            def ljob(nm, start_ac, start_dv=0):
                aT, _dst = LJOBS[nm]
                w(t, "ac", start_ac)
                if start_dv:
                    w(t, "dv", start_dv)
                i = 0
                for lt in range(2):
                    for sc in range(4):
                        if i >= 4:
                            w(t, "dv", dv_off[nm] + i - 4 + 1)
                        for dt in range(NDT):
                            mm = t.matmul(bank(i % 4),
                                          aT[:, dt * 2 * NP + lt * NP:
                                             dt * 2 * NP + (lt + 1) * NP],
                                          xT_sl(dt, sc * 512, 512),
                                          start=(dt == 0),
                                          stop=(dt == NDT - 1))
                        inc(mm, "pe")
                        i += 1

            def rjob(nm, start_ac, start_dv, toks):
                stage, tw = RJOBS[nm]
                nch = max(tw // 512, 1)
                cw = min(tw, 512)
                for tok in toks:
                    wdma(t, tok)
                if start_ac:
                    w(t, "ac", start_ac)
                if start_dv:
                    w(t, "dv", start_dv)
                i = 0
                for ti in range(NDT):
                    for ch in range(nch):
                        if i >= 4:
                            w(t, "ac", ac_off[nm] + (i - 4) + 1)
                        sl = stage[:, ti * tw + ch * cw:
                                   ti * tw + (ch + 1) * cw]
                        t.matmul(bank(i % 4)[:, 0:cw],
                                 rhi_sb[:, ti * NP:(ti + 1) * NP], sl,
                                 start=True, stop=False)
                        mm = t.matmul(bank(i % 4)[:, 0:cw],
                                      rlo_sb[:, ti * NP:(ti + 1) * NP], sl,
                                      start=False, stop=True)
                        inc(mm, "pe")
                        i += 1

            tjob("akT", 0, [M["d_wka"]])
            tjob("avT", ac_off["avT"], [M["d_wva"]])
            ljob("klT", ac_off["avT"])
            ljob("vlT", ac_off["rkf"], D0 + 8)
            rjob("rkf", 0, D0 + 16, M["d_R"] + [M["d_wkb"]])
            tjob("bkT", ac_off["bkT"], [])
            tjob("bvT", ac_off["bvT"], [M["d_wvb"]])
            tjob("woT", ac_off["woT"], [M["d_wo"]])
            rjob("rqf", ac_off["bkT"], 0, [M["d_wq"]])
            tjob("wqT", ac_off["wqT"], [])

        @blk.scalar
        def _(s):
            def tev(nm):
                stage, base, n_ot, d_ext, dst_sl = TJOBS[nm]
                for j in range(d_ext // NP):
                    s.wait_ge(sems["pe"], pe_off[nm] + j + 1)
                    w(s, "ac", cnt["ac"])
                    assert inc(s.copy(
                        dst_sl(j),
                        bankb(4 + 2 * (j % 2), 1)[:, 0:n_ot * NP]),
                        "ac") == ac_off[nm] + j + 1

            def rev(nm):
                stage, tw = RJOBS[nm]
                nch = max(tw // 512, 1)
                cw = min(tw, 512)
                i = 0
                for ti in range(NDT):
                    for ch in range(nch):
                        s.wait_ge(sems["pe"], pe_off[nm] + i + 1)
                        w(s, "ac", cnt["ac"])
                        assert inc(s.copy(
                            stage[:, ti * tw + ch * cw:
                                  ti * tw + (ch + 1) * cw],
                            bank(i % 4)[:, 0:cw]), "ac") == ac_off[nm] + i + 1
                        i += 1

            tev("akT")
            tev("avT")
            rev("rkf")
            tev("bkT")
            tev("bvT")
            tev("woT")
            rev("rqf")
            tev("wqT")

        @blk.vector
        def _(v):
            for nm in ("klT", "vlT"):
                _aT, dst = LJOBS[nm]
                i = 0
                for lt in range(2):
                    for sc in range(4):
                        v.wait_ge(sems["pe"], pe_off[nm] + i + 1)
                        c2 = dvo(v, lambda dst=dst, lt=lt, sc=sc, i=i:
                                 v.tensor_copy(
                                     dst[:, lt * SEQ + sc * 512:
                                         lt * SEQ + (sc + 1) * 512],
                                     bank(i % 4)))
                        assert c2 == dv_off[nm] + i + 1
                        i += 1

    with nc.Block("p1_kq") as blk:
        base_pe, base_ac, base_dv = cnt["pe"], cnt["ac"], cnt["dv"]
        tiles = ([("k", b, c) for b in range(NPAIR) for c in range(4)] +
                 [("q", b, c) for b in range(NPAIR) for c in range(4)] +
                 [("v", st, oc) for st in range(NST) for oc in range(2)])
        NTT = len(tiles)
        PEc = [base_pe + i + 1 for i in range(NTT)]
        ACc, DVc = [None] * NTT, [None] * NTT
        _a, _d = base_ac, base_dv
        for i in range(NTT):
            if i % 2 == 0:
                _a += 1
                ACc[i] = _a
            else:
                _d += 1
                DVc[i] = _d

        def kq_dst(kind, b, c):
            if kind == "k":
                return kT[:, b * SEQ + c * 512: b * SEQ + c * 512 + 512]
            if kind == "v":
                return v_sl(b, c * 512, 512)
            return qT_sl(b, 0, NP, c * 512, 512)

        @blk.tensor
        def _(t):
            for i, (kind, b, c) in enumerate(tiles):
                if i >= 8:
                    j = i - 8
                    if j % 2 == 0:
                        w(t, "ac", ACc[j])
                    else:
                        w(t, "dv", DVc[j])
                bk = bank(i % 8)
                if kind == "k":
                    for lt in range(2):
                        mm = t.matmul(
                            bk,
                            bkT[:, lt * DM + b * NP: lt * DM + (b + 1) * NP],
                            klT[:, lt * SEQ + c * 512: lt * SEQ + (c + 1) * 512],
                            start=(lt == 0), stop=(lt == 1))
                elif kind == "v":
                    for lt in range(2):
                        mm = t.matmul(
                            bk,
                            vlT[:, lt * SEQ + b * NP: lt * SEQ + (b + 1) * NP],
                            bvT[:, lt * DM + c * 512: lt * DM + (c + 1) * 512],
                            start=(lt == 0), stop=(lt == 1))
                else:
                    for dt in range(NDT):
                        mm = t.matmul(
                            bk,
                            wqT[:, dt * DM + b * NP: dt * DM + (b + 1) * NP],
                            xT_sl(dt, c * 512, 512),
                            start=(dt == 0), stop=(dt == NDT - 1))
                assert inc(mm, "pe") == PEc[i]

        @blk.scalar
        def _(s):
            for i, (kind, b, c) in enumerate(tiles):
                if i % 2:
                    continue
                s.wait_ge(sems["pe"], PEc[i])
                w(s, "ac", cnt["ac"])
                assert inc(s.copy(kq_dst(kind, b, c), bank(i % 8)),
                           "ac") == ACc[i]

        @blk.vector
        def _(v):
            for i, (kind, b, c) in enumerate(tiles):
                if i % 2 == 0:
                    continue
                v.wait_ge(sems["pe"], PEc[i])
                c2 = dvo(v, lambda kind=kind, b=b, c=c, i=i: v.tensor_copy(
                    kq_dst(kind, b, c), bank(i % 8)))
                assert c2 == DVc[i]

        @blk.sync
        def _(sy):
            M["d_wfb"] = dma(sy, "dsy",
                             wfb[:].rearrange("p (t d) -> p t d", d=DM),
                             wf8[:, 4:6, :])
            M["d_wfc"] = dma(sy, "dsy",
                             wfc[:].rearrange("p (t d) -> p t d", d=DM),
                             wf8[:, 6:8, :])

    with nc.Block("p1_vwt") as blk:
        base_pe, base_ac, base_dv = cnt["pe"], cnt["ac"], cnt["dv"]

        def wf_t(i):
            if i < 4:
                return wfa[:, i * DM:(i + 1) * DM]
            if i < 6:
                return wfb[:, (i - 4) * DM:(i - 4 + 1) * DM]
            return wfc[:, (i - 6) * DM:(i - 6 + 1) * DM]

        PE_D = base_pe + 1
        PE_A = base_pe + 2
        PE_B = base_pe + 3
        WTR = [base_pe + 3 + i + 1 for i in range(NDT)]
        WEV = [base_ac + i + 1 for i in range(NDT)]
        DV_STAT = base_dv + 9
        DV_DSC = DV_STAT + 1
        DV_DPN = DV_DSC + 2
        DV_ABC = DV_DPN + 1
        WTQ = [DV_ABC + 3 * (i + 1) for i in range(NDT)]

        @blk.tensor
        def _(t):
            w(t, "dv", DV_STAT)
            assert inc(t.matmul(bank(6)[0:1, 0:1], FPSUM, ONESC,
                                start=True, stop=True), "pe") == PE_D
            w(t, "dv", DV_DSC)
            assert inc(t.matmul(bank(6)[:, 4:5], onesf[:], stat_sb[:, 0:1],
                                start=True, stop=True), "pe") == PE_A
            assert inc(t.matmul(bank(6)[:, 8:9], onesf[:], alpha_sb[:],
                                start=True, stop=True), "pe") == PE_B
            for i in range(NDT):
                w(t, "dv", WTQ[i])
                if i >= 2:
                    w(t, "ac", WEV[i - 2])
                bb = bankb(4 + (i % 2), 1)
                for j in range(NDT):
                    mm = t.transpose(bb[:, j * NP:(j + 1) * NP],
                                     wttmp[:, DM + j * NP: DM + (j + 1) * NP],
                                     ident[:])
                assert inc(mm, "pe") == WTR[i]

        @blk.vector
        def _(v):
            wdma(v, M["d_wfa"])
            wdma(v, M["d_wfb"])
            wdma(v, M["d_wfc"])
            for i in range(NDT):
                dvo(v, lambda i=i: v.tensor_reduce(
                    FPART[:, i:i + 1], wf_t(i),
                    AX.X, ALU.add, apply_absolute_value=True))
            c = dvo(v, lambda: v.tensor_reduce(FPSUM, FPART[:, 0:NDT],
                                               AX.X, ALU.add))
            assert c == DV_STAT
            w(v, "pe", PE_D)
            c = dvo(v, lambda: v.tensor_scalar(stat_sb[:, 0:1],
                                               bank(6)[0:1, 0:1],
                                               0.7 / float(DM * DM), None,
                                               ALU.mult))
            assert c == DV_DSC
            w(v, "pe", PE_A)
            dvo(v, lambda: v.tensor_copy(DELTAP, bank(6)[:, 4:5]))
            c = dvo(v, lambda: v.tensor_scalar(DELTAN, DELTAP, -1.0, None,
                                               ALU.mult))
            assert c == DV_DPN
            w(v, "pe", PE_B)
            c = dvo(v, lambda: v.tensor_copy(ALPHA_V, bank(6)[:, 8:9]))
            assert c == DV_ABC
            for i in range(NDT):
                wst = wf_t(i)
                dvo(v, lambda: v.tensor_scalar(wttmp[:, 0:DM], wst,
                                               DELTAP, ALPHA_V,
                                               ALU.is_gt, ALU.mult))
                dvo(v, lambda: v.tensor_scalar(wttmp2[:], wst,
                                               DELTAN, ALPHA_V,
                                               ALU.is_lt, ALU.mult))
                if i >= 1:
                    w(v, "pe", WTR[i - 1])
                c = dvo(v, lambda: v.tensor_tensor(wttmp[:, DM:2 * DM],
                                                   wttmp[:, 0:DM], wttmp2[:],
                                                   ALU.subtract))
                assert c == WTQ[i]

        @blk.scalar
        def _(s):
            for i in range(NDT):
                s.wait_ge(sems["pe"], WTR[i])
                w(s, "ac", cnt["ac"])
                bb = bankb(4 + (i % 2), 1)
                assert inc(s.copy(
                    bass.AP(wtT, i * NP, [[NDT * DM, NP], [DM, NDT], [1, NP]]),
                    bass.AP(psb, bb.offset, [[8 * 1024, NP], [NP, NDT], [1, NP]])),
                    "ac") == WEV[i]

    ST = [bank(0, 2), bank(2, 2)]
    PV = bank(4, 2)
    DEN = bank(6, 2)
    base = {s: cnt[s] for s in ("sc0", "sc1", "ex0", "ex1", "pvs", "eps")}
    assert all(v == 0 for v in base.values())

    def sp_decode(sp):
        sqc, p = divmod(sp, NPAIR)
        return sqc, p

    def emit_pv_step(t, g, start, stop, tail_split=False, eps_sp=None):
        sp, skt = divmod(g, NST)
        sqc, p = sp_decode(sp)
        st_of = [v_sl(skt, (2 * p + h) * HD, HD) for h in range(2)]
        for c in range(2):
            if eps_sp is not None:
                w(t, "eps", 2 * eps_sp - 1 + c)
            rh0 = pring(0)[:, (g % 4) * SQW + c * 512:(g % 4) * SQW + c * 512 + 512]
            rh1 = pring(1)[:, (g % 4) * SQW + c * 512:(g % 4) * SQW + c * 512 + 512]
            t.matmul(PV[0:64, c * 512:(c + 1) * 512], st_of[0], rh0,
                     start=start, stop=stop, tile_position=(0, 0))
            t.matmul(PV[64:128, c * 512:(c + 1) * 512], st_of[1], rh1,
                     start=start, stop=stop, tile_position=(0, 64))
            t.matmul(DEN[0:64, c * 512:(c + 1) * 512], ones_den[:], rh0,
                     start=start, stop=stop, tile_position=(0, 0))
            mm = t.matmul(DEN[64:128, c * 512:(c + 1) * 512], ones_den[:], rh1,
                          start=start, stop=stop, tile_position=(0, 64))
            if tail_split and c == 0:
                inc(mm, "pvt")
        inc(mm, "pvs")

    with nc.Block("p2") as blk:
        @blk.tensor
        def _(t):
            for sp in range(NSP):
                sqc, p = sp_decode(sp)
                for skt in range(NST):
                    g = sp * NST + skt
                    w(t, "ex0", g)
                    for c in range(2):
                        col = sqc * SQW + c * 512
                        mm0 = t.matmul(
                            ST[0][:, c * 512:(c + 1) * 512],
                            kT[0:64, p * SEQ + skt * NP: p * SEQ + (skt + 1) * NP],
                            qT_sl(p, 0, 64, col, 512),
                            start=True, stop=True, tile_position=(0, 0))
                    inc(mm0, "sc0")
                    w(t, "ex1", g)
                    for c in range(2):
                        col = sqc * SQW + c * 512
                        mm1 = t.matmul(
                            ST[1][:, c * 512:(c + 1) * 512],
                            kT[64:128, p * SEQ + skt * NP: p * SEQ + (skt + 1) * NP],
                            qT_sl(p, 64, 64, col, 512),
                            start=True, stop=True, tile_position=(64, 0))
                    inc(mm1, "sc1")
                    if skt == 0 and sp > 0:
                        gp_ = sp * NST - 1
                        w(t, "ex0", gp_ + 1)
                        w(t, "ex1", gp_ + 1)
                        emit_pv_step(t, gp_, start=False, stop=True,
                                     tail_split=True)
                    if skt > 0:
                        emit_pv_step(t, g - 1, start=(skt - 1 == 0),
                                     stop=(skt - 1 == NST - 1 and sp == NSP - 1),
                                     eps_sp=(sp if skt == 1 and sp > 0 else None))
                if sp == NSP - 1:
                    g_last = sp * NST + NST - 1
                    w(t, "ex0", g_last + 1)
                    w(t, "ex1", g_last + 1)
                    emit_pv_step(t, g_last, start=False, stop=True,
                                 tail_split=True)

        @blk.scalar
        def _(s):
            for sp in range(NSP):
                for skt in range(NST):
                    g = sp * NST + skt
                    w(s, "sc0", g + 1)
                    if g >= 4 and g % 2 == 0:
                        w(s, "pvs", g - 2)
                    inc(s.activation(
                        pring(0)[:, (g % 4) * SQW:(g % 4 + 1) * SQW],
                        ST[0], AF.Exp, scale=0.125), "ex0")
                    w(s, "sc1", g + 1)
                    inc(s.activation(
                        pring(1)[:, (g % 4) * SQW:(g % 4 + 1) * SQW],
                        ST[1], AF.Exp, scale=0.125), "ex1")

        @blk.vector
        def _(v):
            for sp in range(NSP):
                sqc, p = sp_decode(sp)
                w(v, "pvt", sp + 1)
                w(v, "dv", cnt["dv"])
                inc(v.tensor_copy(pvden_stage[:, 0:512], bank(4, 1)), "dv")
                inc(v.tensor_copy(pvden_stage[:, SQW:SQW + 512], bank(6, 1)),
                    "eps")
                w(v, "pvs", (sp + 1) * NST)
                inc(v.tensor_copy(pvden_stage[:, 512:SQW], bank(5, 1)), "dv")
                inc(v.tensor_copy(pvden_stage[:, SQW + 512:2 * SQW],
                                  bank(7, 1)), "eps")
                w(v, "eps", 2 * (sp + 1))
                dvo(v, lambda: v.reciprocal(rd_bc[:], pvden_stage[:, SQW:]))
                dvo(v, lambda sp=sp, sqc=sqc, p=p: v.tensor_tensor(
                    OT[:, p * SEQ + sqc * SQW: p * SEQ + (sqc + 1) * SQW],
                    pvden_stage[:, 0:SQW], rd_bc[:], ALU.mult))

    def lnv3(st, k):
        return vec[:, 16 + (st % 3) * 8 + k: 16 + (st % 3) * 8 + k + 1]

    with nc.Block("p3") as blk:
        P3VAR = [None] * NST
        MIDC = [None] * NST
        SQC = [None] * NST
        REC = [None] * NST
        TEVC3 = [None] * NST
        LNRC = [None] * NST
        WOC = [None] * NST
        TRC3 = [None] * NST
        basedv, baseac = cnt["dv"], cnt["ac"]
        _ac = baseac
        _orda = []
        for i in range(NST):
            if i >= 3:
                _orda.append(("cs", i - 3))
            _orda.append(("sq", i))
            if i >= 1:
                _orda.append(("le", i - 1))
            if i >= 4:
                _orda.append(("tv", i - 4))
        _orda += [("le", NST - 1), ("cs", NST - 3), ("cs", NST - 2),
                  ("cs", NST - 1), ("tv", NST - 4), ("tv", NST - 3),
                  ("tv", NST - 2), ("tv", NST - 1)]
        POSTC = [None] * NST
        _lnbf = 0
        for kind, j in _orda:
            if kind == "sq":
                _ac += 1
                SQC[j] = _ac
            elif kind == "le":
                _ac += 2
                REC[j] = _ac
            elif kind == "cs":
                _lnbf += 1
                POSTC[j] = _lnbf
            else:
                _ac += 2
                TEVC3[j] = _ac
        _dv = basedv
        _lnr = 0
        _ordd = []
        for i in range(NST):
            _ordd.append(("pre", i))
            if i >= 1:
                _ordd.append(("mid", i - 1))
            if i >= 2:
                _ordd.append(("post", i - 2))
        _ordd += [("mid", NST - 1), ("post", NST - 2), ("post", NST - 1)]
        PSTC = [None] * NST
        for kind, j in _ordd:
            if kind == "pre":
                _lnr += 1
                LNRC[j] = _lnr
            elif kind == "mid":
                _dv += 4
                MIDC[j] = _dv
            else:
                _dv += 3
                PSTC[j] = _dv
        _wo = 0
        _tr = 0
        _ordp = []
        for i in range(NST):
            _ordp.append(("wo", i))
            if i >= 3:
                _ordp.append(("tr", i - 3))
        _ordp += [("tr", NST - 3), ("tr", NST - 2), ("tr", NST - 1)]
        for kind, j in _ordp:
            if kind == "wo":
                _wo += 1
                WOC[j] = _wo
            else:
                _tr += 1
                TRC3[j] = _tr

        def p3_pair(st):
            return bank(2 * (st % 3), 2)

        def rsl(st):
            return rtile[:, 0:DM] if False else None

        @blk.sync
        def _(sy):
            M["d_xr_t"] = []
            for st in range(NST):
                if st >= 2:
                    sy.wait_ge(sems["lnr"], st - 1)
                M["d_xr_t"].append(dma(
                    sy, "dsy",
                    xres[:, (st % 2) * DM:(st % 2) * DM + DM]
                    .rearrange("p (o d) -> p o d", o=1),
                    x_h.ap().rearrange("(t p) d -> p t d", p=NP)[:, st:st + 1, :]))

        @blk.tensor
        def _(t):
            def emit_wo(st):
                if st >= 3:
                    w(t, "lnr", LNRC[st - 3])
                for oc in range(2):
                    for p in range(NPAIR):
                        mm = t.matmul(
                            p3_pair(st)[:, oc * 512:(oc + 1) * 512],
                            OT[:, p * SEQ + st * NP: p * SEQ + (st + 1) * NP],
                            woT[:, p * DM + oc * 512: p * DM + oc * 512 + 512],
                            start=(p == 0), stop=(p == NPAIR - 1))
                assert inc(mm, "womm") == WOC[st]

            def emit_tr(st):
                w(t, "lnbf", POSTC[st])
                if st >= 2:
                    w(t, "ac", TEVC3[st - 2])
                bb = bankb(6 + (st % 2), 1)
                for j in range(NDT):
                    mm = t.transpose(
                        bb[:, j * NP:(j + 1) * NP],
                        x1bfw[:, (st % 2) * DM + j * NP:
                              (st % 2) * DM + (j + 1) * NP],
                        ident[:])
                assert inc(mm, "x1tr") == TRC3[st]

            for i in range(NST):
                emit_wo(i)
                if i >= 3:
                    emit_tr(i - 3)
            for j in (NST - 3, NST - 2, NST - 1):
                emit_tr(j)

        @blk.vector
        def _(v):
            def pre(st):
                w(v, "womm", WOC[st])
                wdma(v, M["d_xr_t"][st])
                if st >= 3:
                    w(v, "ac", SQC[st - 3])
                r = rtile[:, (st % 3) * DM:(st % 3) * DM + DM]
                w(v, "dv", cnt["dv"])
                assert inc(v.scalar_tensor_tensor(
                    r, p3_pair(st), 1.0,
                    xres[:, (st % 2) * DM:(st % 2) * DM + DM],
                    ALU.mult, ALU.add, accum_out=lnv3(st, 0)),
                    "lnr") == LNRC[st]
                w(v, "lnr", LNRC[st])

            def mid(st):
                v.wait_ge(sems["ac"], SQC[st])
                dvo(v, lambda: v.tensor_scalar(lnv3(st, 2), lnv3(st, 0),
                                               1.0 / DM, None, ALU.mult))
                dvo(v, lambda: v.tensor_scalar(lnv3(st, 3), lnv3(st, 1),
                                               1.0 / DM, EPS, ALU.mult, ALU.add))
                dvo(v, lambda: v.tensor_tensor(lnv3(st, 4), lnv3(st, 2),
                                               lnv3(st, 2), ALU.mult))
                c = dvo(v, lambda: v.tensor_tensor(lnv3(st, 3), lnv3(st, 3),
                                                   lnv3(st, 4), ALU.subtract))
                assert c == MIDC[st]
                P3VAR[st] = c

            def post(st):
                v.wait_ge(sems["ac"], REC[st])
                r = rtile[:, (st % 3) * DM:(st % 3) * DM + DM]
                dvo(v, lambda: v.tensor_tensor(lnv3(st, 6), lnv3(st, 2),
                                               lnv3(st, 5), ALU.mult))
                dvo(v, lambda: v.tensor_scalar(lnv3(st, 6), lnv3(st, 6),
                                               -1.0, None, ALU.mult))
                c = dvo(v, lambda: v.tensor_scalar(x1_t(st), r, lnv3(st, 5),
                                                   lnv3(st, 6), ALU.mult,
                                                   ALU.add))
                assert c == PSTC[st]

            for i in range(NST):
                pre(i)
                if i >= 1:
                    mid(i - 1)
                if i >= 2:
                    post(i - 2)
            mid(NST - 1)
            post(NST - 2)
            post(NST - 1)

        @blk.scalar
        def _(s):
            sqd = sqs[:, 0:DM]

            def sq(st):
                s.wait_ge(sems["lnr"], LNRC[st])
                if st >= 3:
                    s.wait_ge(sems["dv"], MIDC[st - 3])
                r = rtile[:, (st % 3) * DM:(st % 3) * DM + DM]
                w(s, "ac", cnt["ac"])
                assert inc(s.activation(sqd, r, AF.Square,
                                        accum_out=lnv3(st, 1)), "ac") == SQC[st]

            def lnexp(st):
                s.wait_ge(sems["dv"], MIDC[st])
                w(s, "ac", cnt["ac"])
                inc(s.activation(lnv3(st, 5), lnv3(st, 3), AF.Ln), "ac")
                w(s, "ac", cnt["ac"])
                assert inc(s.activation(lnv3(st, 5), lnv3(st, 5), AF.Exp,
                                        scale=-0.5), "ac") == REC[st]

            def cast(st):
                s.wait_ge(sems["dv"], PSTC[st])
                if st >= 2:
                    s.wait_ge(sems["x1tr"], TRC3[st - 2])
                w(s, "ac", cnt["ac"])
                assert inc(s.copy(
                    x1bfw[:, (st % 2) * DM:(st % 2) * DM + DM], x1_t(st)),
                    "lnbf") == POSTC[st]

            def tev(st):
                s.wait_ge(sems["x1tr"], TRC3[st])
                bb = bankb(6 + (st % 2), 1)
                col = st * NP
                w(s, "ac", cnt["ac"])
                inc(s.copy(
                    bass.AP(x1Ta, col, [[4 * SEQ, NP], [SEQ, 4], [1, NP]]),
                    bass.AP(psb, bb.offset, [[8 * 1024, NP], [NP, 4], [1, NP]])),
                    "ac")
                w(s, "ac", cnt["ac"])
                assert inc(s.copy(
                    bass.AP(x1Tb, col, [[4 * SEQ, NP], [SEQ, 4], [1, NP]]),
                    bass.AP(psb, bb.offset + 4 * NP,
                            [[8 * 1024, NP], [NP, 4], [1, NP]])),
                    "ac") == TEVC3[st]

            for i in range(NST):
                if i >= 3:
                    cast(i - 3)
                sq(i)
                if i >= 1:
                    lnexp(i - 1)
                if i >= 4:
                    tev(i - 4)
            lnexp(NST - 1)
            for j in (NST - 3, NST - 2, NST - 1):
                cast(j)
            for j in (NST - 4, NST - 3, NST - 2, NST - 1):
                tev(j)

    with nc.Block("p4") as blk:
        P4VAR = [None] * NST
        P4PST = [None] * NST
        MID4 = [None] * NST
        SQ4 = [None] * NST
        RE4 = [None] * NST
        FFC = [None] * NST
        basedv4, baseac4 = cnt["dv"], cnt["ac"]
        _ac = baseac4
        for i in range(NST):
            _ac += 1
            SQ4[i] = _ac
            if i >= 1:
                _ac += 2
                RE4[i - 1] = _ac
        RE4[NST - 1] = _ac + 2
        _rr = dma_rr["dsy"]
        _pc = {i: cnt[f"dsy{i}"] for i in range(NDMASEM)}
        M["d_out_t"] = []
        for st in range(NST):
            i = (_rr + st) % NDMASEM
            _pc[i] += 16
            M["d_out_t"].append((f"dsy{i}", _pc[i]))

        r2ring = place("r2ring", [NP, 3 * DM], F32, 150 * KB)
        sqd4 = place("sqd4", [NP, DM], F32, 162 * KB)
        lnv4t = lnv3

        def r2sl(st):
            return r2ring[:, (st % 3) * DM:(st % 3) * DM + DM]

        @blk.tensor
        def _(t):
            wdma(t, M["d_blo"])
            for st in range(NST):
                if st >= 4:
                    w(t, "ffnr", st - 3)
                for oc in range(2):
                    t.matmul(bank(2 * (st % 4) + oc), ones2[:],
                             bias_hl[:, oc * 512:(oc + 1) * 512],
                             start=True, stop=False)
                    for dt in range(NDT):
                        xs = x1Ta if dt < 4 else x1Tb
                        mm = t.matmul(
                            bank(2 * (st % 4) + oc),
                            xs[:, (dt % 4) * SEQ + st * NP:
                               (dt % 4) * SEQ + (st + 1) * NP],
                            wtT[:, dt * DM + oc * 512: dt * DM + oc * 512 + 512],
                            start=False, stop=(dt == NDT - 1))
                inc(mm, "ffnm")

        @blk.vector
        def _(v):
            def pre(st):
                w(v, "ffnm", st + 1)
                if st >= 3:
                    w(v, "ac", SQ4[st - 3])
                    w(v, "ln2ap", st - 2)
                r2 = r2sl(st)
                w(v, "dv", cnt["dv"])
                inc(v.scalar_tensor_tensor(
                    r2, bank(2 * (st % 4), 2), 1.0,
                    x1_t(st), ALU.mult, ALU.add, accum_out=lnv4t(st, 0)),
                    "ffnr")
                w(v, "ffnr", st + 1)

            def mid(st):
                v.wait_ge(sems["ac"], SQ4[st])
                dvo(v, lambda: v.tensor_scalar(lnv4t(st, 2), lnv4t(st, 0),
                                               1.0 / DM, None, ALU.mult))
                dvo(v, lambda: v.tensor_scalar(lnv4t(st, 3), lnv4t(st, 1),
                                               1.0 / DM, EPS, ALU.mult, ALU.add))
                dvo(v, lambda: v.tensor_scalar(lnv4t(st, 4), lnv4t(st, 2),
                                               lnv4t(st, 2), None, ALU.mult))
                P4VAR[st] = dvo(v, lambda: v.tensor_tensor(
                    lnv4t(st, 3), lnv4t(st, 3), lnv4t(st, 4), ALU.subtract))
                MID4[st] = P4VAR[st]

            def post(st):
                v.wait_ge(sems["ac"], RE4[st])
                r2 = r2sl(st)
                dvo(v, lambda: v.tensor_tensor(lnv4t(st, 6), lnv4t(st, 2),
                                               lnv4t(st, 5), ALU.mult))
                dvo(v, lambda: v.tensor_scalar(lnv4t(st, 6), lnv4t(st, 6),
                                               -1.0, None, ALU.mult))
                if st >= 2:
                    wdma(v, M["d_out_t"][st - 2])
                w(v, "dv", cnt["dv"])
                inc(v.tensor_scalar(
                    ostage[:, (st % 2) * DM:(st % 2) * DM + DM],
                    r2, lnv4t(st, 5), lnv4t(st, 6), ALU.mult, ALU.add), "ln2ap")

            for i in range(NST):
                pre(i)
                if i >= 1:
                    mid(i - 1)
                if i >= 2:
                    post(i - 2)
            mid(NST - 1)
            post(NST - 2)
            post(NST - 1)

        @blk.scalar
        def _(s):
            def sq(st):
                s.wait_ge(sems["ffnr"], st + 1)
                if st >= 3:
                    s.wait_ge(sems["dv"], MID4[st - 3])
                w(s, "ac", cnt["ac"])
                assert inc(s.activation(sqd4[:], r2sl(st), AF.Square,
                                        accum_out=lnv4t(st, 1)), "ac") == SQ4[st]

            def lnexp(st):
                s.wait_ge(sems["dv"], P4VAR[st])
                w(s, "ac", cnt["ac"])
                inc(s.activation(lnv4t(st, 5), lnv4t(st, 3), AF.Ln), "ac")
                w(s, "ac", cnt["ac"])
                assert inc(s.activation(lnv4t(st, 5), lnv4t(st, 5), AF.Exp,
                                        scale=-0.5), "ac") == RE4[st]

            for i in range(NST):
                sq(i)
                if i >= 1:
                    lnexp(i - 1)
            lnexp(NST - 1)

        @blk.sync
        def _(sy):
            for st in range(NST):
                sy.wait_ge(sems["ln2ap"], st + 1)
                tok = dma(
                    sy, "dsy",
                    out_h.ap().rearrange("(t p) d -> p t d", p=NP)[:, st:st + 1, :],
                    ostage[:, (st % 2) * DM:(st % 2) * DM + DM]
                    .rearrange("p (o d) -> p o d", o=1))
                assert tok == M["d_out_t"][st], (tok, M["d_out_t"][st])
            for i in range(NDMASEM):
                sy.wait_ge(sems[f"dsy{i}"], cnt[f"dsy{i}"])

    return nc


_PROG_CACHE = {}


def _get_program():
    if "nc" not in _PROG_CACHE:
        _PROG_CACHE["nc"] = build_program()
    return _PROG_CACHE["nc"]


def _rot_matrices():
    inv = 1.0 / (10000.0 ** (np.arange(0, HD, 2, dtype=np.float64)[: DROPE // 2] / HD))
    R = np.zeros((NP, NPAIR * NP), dtype=np.float64)
    for pb in range(NPAIR):
        blk = np.eye(NP, dtype=np.float64)
        for j in range(2):
            h = 2 * pb + j
            for k in range(DROPE // 2):
                c, s = np.cos(h * inv[k]), np.sin(h * inv[k])
                e, o = j * HD + 2 * k, j * HD + 2 * k + 1
                blk[e, e] = c
                blk[o, e] = -s
                blk[e, o] = s
                blk[o, o] = c
        R[:, pb * NP:(pb + 1) * NP] = blk
    hi = R.astype(ml_dtypes.bfloat16)
    lo = (R - hi.astype(np.float64)).astype(ml_dtypes.bfloat16)
    return np.ascontiguousarray(hi), np.ascontiguousarray(lo)


def kernel(L0_attn_x, L0_attn_wq, L0_attn_wk_a, L0_attn_wk_b, L0_attn_wv_a,
           L0_attn_wv_b, L0_attn_wo, L0_ffn_weight, L0_ffn_alpha, L0_ffn_bias):
    nc = _get_program()
    rhi, rlo = _rot_matrices()
    ident = np.eye(NP, dtype=ml_dtypes.bfloat16)
    common = {
        "wq": np.ascontiguousarray(L0_attn_wq, dtype=np.float32),
        "wka": np.ascontiguousarray(L0_attn_wk_a, dtype=np.float32),
        "wkb": np.ascontiguousarray(L0_attn_wk_b, dtype=np.float32),
        "wva": np.ascontiguousarray(L0_attn_wv_a, dtype=np.float32),
        "wvb": np.ascontiguousarray(L0_attn_wv_b, dtype=np.float32),
        "wo": np.ascontiguousarray(L0_attn_wo, dtype=np.float32),
        "wf": np.ascontiguousarray(L0_ffn_weight, dtype=np.float32),
        "alpha": np.asarray(L0_ffn_alpha, dtype=np.float32).reshape(1, 1),
        "fbias": np.asarray(L0_ffn_bias, dtype=np.float32).reshape(1, DM),
        "identi": ident,
        "rhi": rhi, "rlo": rlo,
    }
    x = np.asarray(L0_attn_x, dtype=np.float32)
    in_maps = [dict(common, x=np.ascontiguousarray(x[c])) for c in range(8)]
    res = run_bass_kernel_spmd(nc, in_maps, core_ids=list(range(8)))
    out = np.stack([np.asarray(res.results[c]["out"]) for c in range(8)], axis=0)
    return out.astype(np.float32)


# version 34 (final)
# speedup vs baseline: 1.0202x; 1.0083x over previous
import numpy as np
import ml_dtypes
import concourse.bass as bass
import concourse.mybir as mybir
from concourse.bass_utils import run_bass_kernel_spmd

F32 = mybir.dt.float32
BF16 = mybir.dt.bfloat16
AF = mybir.ActivationFunctionType
ALU = mybir.AluOpType
AX = mybir.AxisListType

SEQ, DM, NH, HD, DROPE, LAT = 2048, 1024, 16, 64, 16, 256
NP = 128
NST = SEQ // NP
NDT = DM // NP
NPAIR = NH // 2
SQW = 1024
NSQC = SEQ // SQW
NSP = NSQC * NPAIR
EPS = 1e-5
KB = 1024


def build_program():
    nc = bass.Bass("TRN2", target_bir_lowering=False)

    x_h = nc.dram_tensor("x", [SEQ, DM], F32, kind="ExternalInput")
    wq_h = nc.dram_tensor("wq", [DM, DM], F32, kind="ExternalInput")
    wka_h = nc.dram_tensor("wka", [LAT, DM], F32, kind="ExternalInput")
    wkb_h = nc.dram_tensor("wkb", [DM, LAT], F32, kind="ExternalInput")
    wva_h = nc.dram_tensor("wva", [LAT, DM], F32, kind="ExternalInput")
    wvb_h = nc.dram_tensor("wvb", [DM, LAT], F32, kind="ExternalInput")
    wo_h = nc.dram_tensor("wo", [DM, DM], F32, kind="ExternalInput")
    wf_h = nc.dram_tensor("wf", [DM, DM], F32, kind="ExternalInput")
    alpha_h = nc.dram_tensor("alpha", [1, 1], F32, kind="ExternalInput")
    bias_h = nc.dram_tensor("fbias", [1, DM], F32, kind="ExternalInput")
    ident_h = nc.dram_tensor("identi", [NP, NP], BF16, kind="ExternalInput")
    rhi_h = nc.dram_tensor("rhi", [NP, NPAIR * NP], BF16, kind="ExternalInput")
    rlo_h = nc.dram_tensor("rlo", [NP, NPAIR * NP], BF16, kind="ExternalInput")
    out_h = nc.dram_tensor("out", [SEQ, DM], F32, kind="ExternalOutput")

    ARENA = 206 * KB
    arena = nc.alloc_sbuf_tensor("arena", [NP, ARENA], mybir.dt.uint8)
    arena_base = nc.lookup_mloc(arena).addr

    def place(name, shape, dtype, off):
        per_part = int(np.prod(shape[1:])) * mybir.dt.size(dtype)
        assert off % 32 == 0 and off + per_part <= ARENA, (name, off, per_part)
        h = nc.virtual_sbuf_tensor(name, list(shape), dtype)
        nc._tensor(h.name, list(h.shape), h.dtype, type="SB")
        mloc = nc.lookup_mloc(h)
        mloc.allocated = not nc.target_bir_lowering
        mloc.addr = arena_base + off
        return h

    ident = place("identb", [NP, NP], BF16, 0)
    ones_den = place("ones_den", [NP, HD], BF16, 256)
    onesf = place("onesf", [1, NP], F32, 384)
    vec = place("vec", [NP, 64], F32, 1 * KB)
    bias_row = place("bias_row", [1, DM], F32, 1 * KB + 256)
    alpha_sb = place("alpha_sb", [1, 1], F32, 5 * KB + 256)
    stat_sb = place("stat_sb", [1, 8], F32, 5 * KB + 288)
    rd_bc = place("rd_bc", [NP, SQW], F32, 5 * KB + 512)
    x1bfw = place("x1bfw", [NP, 2 * DM], BF16, 5 * KB + 512)
    ones2 = place("ones2", [2, NP], BF16, 9 * KB + 512)
    bias_hl = place("bias_hl", [2, DM], BF16, 10 * KB)
    bias_lo = place("bias_lo", [1, DM], BF16, 12 * KB)
    woT = place("woT", [NP, NDT * DM], BF16, 14 * KB)
    wqstage = place("wqstage", [NP, NDT * DM], BF16, 78 * KB)
    xTb = place("xTb", [NP, 4 * SEQ], BF16, 30 * KB)
    wtT = place("wtT", [NP, NDT * DM], BF16, 30 * KB)
    wfa = place("wfa", [NP, 4 * DM], F32, 142 * KB)
    wfb = place("wfb", [NP, 2 * DM], F32, 174 * KB)
    wfc = place("wfc", [NP, 2 * DM], F32, 110 * KB)
    akT = place("akT", [NP, NDT * 2 * NP], BF16, 46 * KB)
    avT = place("avT", [NP, NDT * 2 * NP], BF16, 50 * KB)
    bkT = place("bkT", [NP, 2 * DM], BF16, 54 * KB)
    bvT = place("bvT", [NP, 2 * DM], BF16, 58 * KB)
    qTb = place("qTb", [NP, 2 * SEQ], BF16, 46 * KB)
    pring1 = place("pring1", [NP, 4 * SQW], BF16, 54 * KB)
    x1Ta = place("x1Ta", [NP, 4 * SEQ], BF16, 46 * KB)
    wostage = place("wostage", [NP, NDT * DM], BF16, 62 * KB)
    wqT = place("wqT", [NP, NDT * DM], BF16, 62 * KB)
    vfa = place("vfa", [NP, 8 * DM], BF16, 62 * KB)
    x1Tb = place("x1Tb", [NP, 4 * SEQ], BF16, 62 * KB)
    kT = place("kT", [NP, NDT * SEQ], BF16, 78 * KB)
    x1a = place("x1a", [NP, 8 * DM], F32, 78 * KB)
    wkbstage = place("wkbstage", [NP, 2 * DM], BF16, 94 * KB)
    wvbstage = place("wvbstage", [NP, 2 * DM], BF16, 98 * KB)
    kqstage = place("kqstage", [NP, 4 * DM], BF16, 110 * KB)
    pvden_stage = place("pvden_stage", [NP, 2 * SQW], F32, 110 * KB)
    wttmp = place("wttmp", [NP, 2 * DM], BF16, 198 * KB)
    wttmp2 = place("wttmp2", [NP, DM], BF16, 202 * KB)
    qTa = place("qTa", [NP, 6 * SEQ], BF16, 118 * KB)
    x1b = place("x1b", [NP, 8 * DM], F32, 110 * KB)
    xbf = place("xbf", [NP, NST * DM], BF16, 142 * KB)
    rhi_sb = place("rhi_sb", [NP, NPAIR * NP], BF16, 118 * KB)
    rlo_sb = place("rlo_sb", [NP, NPAIR * NP], BF16, 120 * KB)
    klT = place("klT", [NP, 2 * SEQ], BF16, 158 * KB)
    vlT = place("vlT", [NP, 2 * SEQ], BF16, 166 * KB)
    OT = place("OT", [NP, NPAIR * SEQ], BF16, 142 * KB)
    ttile = place("ttile", [NP, 2 * DM], F32, 142 * KB)
    r2tile = place("r2tile", [NP, 2 * DM], F32, 150 * KB)
    pring0 = place("pring0", [NP, 4 * SQW], BF16, 174 * KB)
    xTa = place("xTa", [NP, 4 * SEQ], BF16, 182 * KB)
    vfb = place("vfb", [NP, 8 * DM], BF16, 182 * KB)
    xres = place("xres", [NP, 2 * DM], F32, 174 * KB)
    rtile = place("rtile", [NP, 3 * DM], F32, 182 * KB)
    sqs = place("sqs", [NP, DM], F32, 194 * KB)
    ostage = place("ostage", [NP, 2 * DM], F32, 198 * KB)

    def xT_sl(dt, col, width):
        t = xTa if dt < 4 else xTb
        return t[:, (dt % 4) * SEQ + col: (dt % 4) * SEQ + col + width]

    def qT_sl(p, p0, nparts, col, width):
        t, i = (qTa, p) if p < 6 else (qTb, p - 6)
        return t[p0:p0 + nparts, i * SEQ + col: i * SEQ + col + width]

    def v_sl(st, col, width):
        t = vfa if st < 8 else vfb
        return t[:, (st % 8) * DM + col: (st % 8) * DM + col + width]

    def x1_t(st):
        t = x1a if st < 8 else x1b
        return t[:, (st % 8) * DM: (st % 8) * DM + DM]

    def pring(h):
        return pring0 if h == 0 else pring1

    ps = nc.psum_tensor("ps", [NP, 8 * 512], F32).__enter__()
    psb = nc.virtual_psum_tensor("psb", [NP, 8 * 1024], BF16)
    nc._tensor(psb.name, list(psb.shape), psb.dtype, type="PSUM")
    _pm = nc.lookup_mloc(psb)
    _pm.allocated = not nc.target_bir_lowering
    _pm.addr = nc.lookup_mloc(ps).addr

    def bank(i, n=1):
        return ps[:, i * 512: (i + n) * 512]

    def bankb(i, n=1):
        return psb[:, i * 1024: (i + n) * 1024]

    cnt = {}
    sems = {}
    for n in ("pe", "dv", "ac", "gp",
              "sc0", "sc1", "ex0", "ex1", "pvs", "eps", "pvt",
              "womm", "lnr", "lnbf", "x1tr", "x1ev",
              "ffnm", "ffne", "ffnr", "ln2ap", "gpt"):
        sems[n] = nc.semaphore(n).__enter__()
        cnt[n] = 0
    NDMASEM = 8
    for q in ("dgp", "dsy"):
        for i in range(NDMASEM):
            n = f"{q}{i}"
            sems[n] = nc.semaphore(n).__enter__()
            cnt[n] = 0
    dma_rr = {"dgp": 0, "dsy": 0}

    def inc(instr, s, n=1):
        instr.then_inc(sems[s], n)
        cnt[s] += n
        return cnt[s]

    def w(eng, s, val):
        if val > 0:
            eng.wait_ge(sems[s], val)

    def dma(q, pool, dst, src, transpose=False):
        i = dma_rr[pool] % NDMASEM
        dma_rr[pool] += 1
        s = f"{pool}{i}"
        inc(q.dma_start(dst, src, transpose=transpose), s, 16)
        return (s, cnt[s])

    def wdma(eng, tok):
        eng.wait_ge(sems[tok[0]], tok[1])

    def dvo(v, emit):
        w(v, "dv", cnt["dv"])
        return inc(emit(), "dv")

    def aco(a, emit):
        w(a, "ac", cnt["ac"])
        return inc(emit(), "ac")

    def gpo(g, emit):
        w(g, "gp", cnt["gp"])
        return inc(emit(), "gp")

    def vs(i):
        return vec[:, i:i + 1]

    ALPHA_V, DELTAP, DELTAN, ONESC = vs(0), vs(1), vs(2), vs(3)
    FPART = vec[:, 4:12]
    FPSUM = vs(12)

    def lnv(st, k):
        i = 16 + (st % 2) * 8 + k
        return vec[:, i:i + 1]

    M = {}

    def transpose_job(name, stage, stage_base, n_ot, d_ext, dst_sl,
                      wait_tok=None):
        n_dt = d_ext // NP
        base_pe, base_ac = cnt["pe"], cnt["ac"]

        with nc.Block(name) as blk:
            @blk.tensor
            def _(t):
                if wait_tok is not None:
                    wdma(t, wait_tok)
                for j in range(n_dt):
                    if j >= 2:
                        w(t, "ac", base_ac + (j - 2) + 1)
                    bb = bankb(4 + 2 * (j % 2), 1)
                    for i in range(n_ot):
                        c0 = stage_base + i * d_ext + j * NP
                        mm = t.transpose(bb[:, i * NP:(i + 1) * NP],
                                         stage[:, c0:c0 + NP], ident[:])
                    inc(mm, "pe")

            @blk.scalar
            def _(s):
                for j in range(n_dt):
                    w(s, "pe", base_pe + j + 1)
                    w(s, "ac", cnt["ac"])
                    inc(s.copy(dst_sl(j),
                               bankb(4 + 2 * (j % 2), 1)[:, 0:n_ot * NP]), "ac")

    with nc.Block("p00", no_gpsimd_drain=True) as blk:
        @blk.scalar
        def _(s):
            from concourse.hw_specs import get_activation_tables
            names = list(get_activation_tables(nc.m.arch).keys())
            s.add_instruction(mybir.InstLoadActFuncSet(
                name=nc.get_next_instruction_name(),
                act_func_set_id=names.index("natural_log_exp_and_others"),
                ins=[], outs=[]))

        @blk.vector
        def _(v):
            dvo(v, lambda: v.memset(ones_den[:], 1.0))
            dvo(v, lambda: v.memset(onesf[:], 1.0))
            dvo(v, lambda: v.memset(ONESC, 1.0))
            dvo(v, lambda: v.memset(ones2[:], 1.0))
            M["dv_const"] = cnt["dv"]

        @blk.gpsimd
        def _(g):
            M["d_x"] = [
                dma(g, "dgp",
                    xbf[:, h * 8 * DM:(h + 1) * 8 * DM]
                    .rearrange("p (t d) -> p t d", d=DM),
                    x_h.ap().rearrange("(t p) d -> p t d", p=NP)[:, h * 8:
                                                                (h + 1) * 8, :])
                for h in range(2)]
            M["d_wka"] = dma(g, "dgp",
                             kqstage[:, 0:2 * DM].rearrange(
                                 "p (t d) -> p t d", d=DM),
                             wka_h.ap().rearrange("(t p) d -> p t d", p=NP))
            M["d_wva"] = dma(g, "dgp",
                             kqstage[:, 2 * DM:4 * DM].rearrange(
                                 "p (t d) -> p t d", d=DM),
                             wva_h.ap().rearrange("(t p) d -> p t d", p=NP))
            M["d_wkb"] = dma(g, "dgp",
                             wkbstage[:].rearrange("p (t d) -> p t d", d=LAT),
                             wkb_h.ap().rearrange("(t p) d -> p t d", p=NP))
            M["d_wvb"] = dma(g, "dgp",
                             wvbstage[:].rearrange("p (t d) -> p t d", d=LAT),
                             wvb_h.ap().rearrange("(t p) d -> p t d", p=NP))
            M["d_wo"] = dma(g, "dgp",
                            wostage[:].rearrange("p (t d) -> p t d", d=DM),
                            wo_h.ap().rearrange("(t p) d -> p t d", p=NP))
            M["d_wq"] = dma(g, "dgp",
                            wqstage[:].rearrange("p (t d) -> p t d", d=DM),
                            wq_h.ap().rearrange("(t p) d -> p t d", p=NP))

        @blk.sync
        def _(sy):
            t1 = dma(sy, "dsy", ident[:], ident_h.ap())
            t2 = dma(sy, "dsy", alpha_sb[:], alpha_h.ap())
            t3 = dma(sy, "dsy", bias_row[:], bias_h.ap())
            M["d_small"] = [t1, t2, t3]
            M["d_R"] = [dma(sy, "dsy", rhi_sb[:], rhi_h.ap()),
                        dma(sy, "dsy", rlo_sb[:], rlo_h.ap())]

    with nc.Block("p1_xT") as blk:
        base_pe, base_dv = cnt["pe"], cnt["dv"]

        @blk.tensor
        def _(t):
            for h in range(2):
                wdma(t, M["d_x"][h])
                if h == 0:
                    for tok in M["d_small"]:
                        wdma(t, tok)
                for dt in range(NDT):
                    if dt >= 2:
                        w(t, "dv", base_dv + h * NDT + (dt - 2) + 1)
                    bb = bankb(4 + 2 * (dt % 2) + h, 1)
                    for j in range(8):
                        st = h * 8 + j
                        mm = t.transpose(
                            bb[:, j * NP:(j + 1) * NP],
                            xbf[:, st * DM + dt * NP: st * DM + (dt + 1) * NP],
                            ident[:])
                    inc(mm, "pe")

        @blk.vector
        def _(v):
            for h in range(2):
                for dt in range(NDT):
                    w(v, "pe", base_pe + h * NDT + dt + 1)
                    dvo(v, lambda h=h, dt=dt: v.tensor_copy(
                        xT_sl(dt, h * 1024, 1024),
                        bankb(4 + 2 * (dt % 2) + h, 1)))

    wf8 = wf_h.ap().rearrange("(t p) d -> p t d", p=NP)
    with nc.Block("p1_tab") as blk:
        @blk.sync
        def _(sy):
            M["d_wfa"] = dma(sy, "dsy",
                             wfa[:].rearrange("p (t d) -> p t d", d=DM),
                             wf8[:, 0:4, :])

        @blk.vector
        def _(v):
            wdma(v, M["d_small"][2])
            dvo(v, lambda: v.tensor_copy(bias_hl[0:1, :], bias_row[:]))
            dvo(v, lambda: v.tensor_tensor(bias_lo[:], bias_row[:],
                                           bias_hl[0:1, :], ALU.subtract))
            M["dv_blo"] = cnt["dv"]

        @blk.gpsimd
        def _(g):
            w(g, "dv", M["dv_blo"])
            M["d_blo"] = dma(g, "dgp", bias_hl[1:2, :], bias_lo[:])

    with nc.Block("p1_jobs") as blk:
        P0, A0, D0 = cnt["pe"], cnt["ac"], cnt["dv"]
        pe_off, ac_off = {}, {}
        _o = P0
        for nm, n in (("akT", 8), ("avT", 8), ("klT", 8), ("vlT", 8),
                      ("rkf", 8), ("bkT", 2), ("bvT", 2), ("woT", 8),
                      ("rqf", 16), ("wqT", 8)):
            pe_off[nm] = _o
            _o += n
        _o = A0
        for nm, n in (("akT", 8), ("avT", 8), ("rkf", 8), ("bkT", 2),
                      ("bvT", 2), ("woT", 8), ("rqf", 16), ("wqT", 8)):
            ac_off[nm] = _o
            _o += n
        dv_off = {"klT": D0, "vlT": D0 + 8}

        TJOBS = {
            "akT": (kqstage, 0, 2, DM, 0,
                    lambda j: akT[:, j * 2 * NP:(j + 1) * 2 * NP]),
            "avT": (kqstage, 2 * DM, 2, DM, 1,
                    lambda j: avT[:, j * 2 * NP:(j + 1) * 2 * NP]),
            "bkT": (wkbstage, 0, NDT, LAT, 0,
                    lambda j: bkT[:, j * DM:(j + 1) * DM]),
            "bvT": (wvbstage, 0, NDT, LAT, 1,
                    lambda j: bvT[:, j * DM:(j + 1) * DM]),
            "woT": (wostage, 0, NDT, DM, 0,
                    lambda j: woT[:, j * DM:(j + 1) * DM]),
            "wqT": (wqstage, 0, NDT, DM, 1,
                    lambda j: wqT[:, j * DM:(j + 1) * DM]),
        }
        LJOBS = {"klT": (akT, klT), "vlT": (avT, vlT)}
        RJOBS = {"rkf": (wkbstage, LAT), "rqf": (wqstage, DM)}

        @blk.tensor
        def _(t):
            def tjob(nm, start_ac, toks):
                stage, base, n_ot, d_ext, pair, _sl = TJOBS[nm]
                for tok in toks:
                    wdma(t, tok)
                w(t, "ac", start_ac)
                for j in range(d_ext // NP):
                    if j >= 2:
                        w(t, "ac", ac_off[nm] + (j - 2) + 1)
                    bb = bankb(4 + pair + 2 * (j % 2), 1)
                    for i in range(n_ot):
                        c0 = base + i * d_ext + j * NP
                        mm = t.transpose(bb[:, i * NP:(i + 1) * NP],
                                         stage[:, c0:c0 + NP], ident[:])
                    inc(mm, "pe")

            def ljob(nm, start_ac, start_dv=0):
                aT, _dst = LJOBS[nm]
                w(t, "ac", start_ac)
                if start_dv:
                    w(t, "dv", start_dv)
                i = 0
                for lt in range(2):
                    for sc in range(4):
                        if i >= 4:
                            w(t, "dv", dv_off[nm] + i - 4 + 1)
                        for dt in range(NDT):
                            mm = t.matmul(bank(i % 4),
                                          aT[:, dt * 2 * NP + lt * NP:
                                             dt * 2 * NP + (lt + 1) * NP],
                                          xT_sl(dt, sc * 512, 512),
                                          start=(dt == 0),
                                          stop=(dt == NDT - 1))
                        inc(mm, "pe")
                        i += 1

            def rjob(nm, start_ac, start_dv, toks):
                stage, tw = RJOBS[nm]
                nch = max(tw // 512, 1)
                cw = min(tw, 512)
                for tok in toks:
                    wdma(t, tok)
                if start_ac:
                    w(t, "ac", start_ac)
                if start_dv:
                    w(t, "dv", start_dv)
                i = 0
                for ti in range(NDT):
                    for ch in range(nch):
                        if i >= 4:
                            w(t, "ac", ac_off[nm] + (i - 4) + 1)
                        sl = stage[:, ti * tw + ch * cw:
                                   ti * tw + (ch + 1) * cw]
                        t.matmul(bank(i % 4)[:, 0:cw],
                                 rhi_sb[:, ti * NP:(ti + 1) * NP], sl,
                                 start=True, stop=False)
                        mm = t.matmul(bank(i % 4)[:, 0:cw],
                                      rlo_sb[:, ti * NP:(ti + 1) * NP], sl,
                                      start=False, stop=True)
                        inc(mm, "pe")
                        i += 1

            tjob("akT", 0, [M["d_wka"]])
            tjob("avT", 0, [M["d_wva"]])
            ljob("klT", ac_off["avT"])
            ljob("vlT", ac_off["rkf"], D0 + 8)
            rjob("rkf", 0, D0 + 16, M["d_R"] + [M["d_wkb"]])
            tjob("bkT", ac_off["bkT"], [])
            tjob("bvT", ac_off["rkf"], [M["d_wvb"]])
            tjob("woT", ac_off["bvT"], [M["d_wo"]])
            rjob("rqf", ac_off["bkT"], 0, [M["d_wq"]])
            tjob("wqT", ac_off["wqT"], [])

        @blk.scalar
        def _(s):
            def tev(nm):
                stage, base, n_ot, d_ext, pair, dst_sl = TJOBS[nm]
                for j in range(d_ext // NP):
                    s.wait_ge(sems["pe"], pe_off[nm] + j + 1)
                    w(s, "ac", cnt["ac"])
                    assert inc(s.copy(
                        dst_sl(j),
                        bankb(4 + pair + 2 * (j % 2), 1)[:, 0:n_ot * NP]),
                        "ac") == ac_off[nm] + j + 1

            def rev(nm):
                stage, tw = RJOBS[nm]
                nch = max(tw // 512, 1)
                cw = min(tw, 512)
                i = 0
                for ti in range(NDT):
                    for ch in range(nch):
                        s.wait_ge(sems["pe"], pe_off[nm] + i + 1)
                        w(s, "ac", cnt["ac"])
                        assert inc(s.copy(
                            stage[:, ti * tw + ch * cw:
                                  ti * tw + (ch + 1) * cw],
                            bank(i % 4)[:, 0:cw]), "ac") == ac_off[nm] + i + 1
                        i += 1

            tev("akT")
            tev("avT")
            rev("rkf")
            tev("bkT")
            tev("bvT")
            tev("woT")
            rev("rqf")
            tev("wqT")

        @blk.vector
        def _(v):
            for nm in ("klT", "vlT"):
                _aT, dst = LJOBS[nm]
                i = 0
                for lt in range(2):
                    for sc in range(4):
                        v.wait_ge(sems["pe"], pe_off[nm] + i + 1)
                        c2 = dvo(v, lambda dst=dst, lt=lt, sc=sc, i=i:
                                 v.tensor_copy(
                                     dst[:, lt * SEQ + sc * 512:
                                         lt * SEQ + (sc + 1) * 512],
                                     bank(i % 4)))
                        assert c2 == dv_off[nm] + i + 1
                        i += 1

    with nc.Block("p1_kq") as blk:
        base_pe, base_ac, base_dv = cnt["pe"], cnt["ac"], cnt["dv"]
        tiles = ([("k", b, c) for b in range(NPAIR) for c in range(4)] +
                 [("q", b, c) for b in range(NPAIR) for c in range(4)] +
                 [("v", st, oc) for st in range(NST) for oc in range(2)])
        NTT = len(tiles)
        PEc = [base_pe + i + 1 for i in range(NTT)]
        ACc, DVc = [None] * NTT, [None] * NTT
        _a, _d = base_ac, base_dv
        for i in range(NTT):
            if i % 2 == 0:
                _a += 1
                ACc[i] = _a
            else:
                _d += 1
                DVc[i] = _d

        def kq_dst(kind, b, c):
            if kind == "k":
                return kT[:, b * SEQ + c * 512: b * SEQ + c * 512 + 512]
            if kind == "v":
                return v_sl(b, c * 512, 512)
            return qT_sl(b, 0, NP, c * 512, 512)

        @blk.tensor
        def _(t):
            for i, (kind, b, c) in enumerate(tiles):
                if i >= 8:
                    j = i - 8
                    if j % 2 == 0:
                        w(t, "ac", ACc[j])
                    else:
                        w(t, "dv", DVc[j])
                bk = bank(i % 8)
                if kind == "k":
                    for lt in range(2):
                        mm = t.matmul(
                            bk,
                            bkT[:, lt * DM + b * NP: lt * DM + (b + 1) * NP],
                            klT[:, lt * SEQ + c * 512: lt * SEQ + (c + 1) * 512],
                            start=(lt == 0), stop=(lt == 1))
                elif kind == "v":
                    for lt in range(2):
                        mm = t.matmul(
                            bk,
                            vlT[:, lt * SEQ + b * NP: lt * SEQ + (b + 1) * NP],
                            bvT[:, lt * DM + c * 512: lt * DM + (c + 1) * 512],
                            start=(lt == 0), stop=(lt == 1))
                else:
                    for dt in range(NDT):
                        mm = t.matmul(
                            bk,
                            wqT[:, dt * DM + b * NP: dt * DM + (b + 1) * NP],
                            xT_sl(dt, c * 512, 512),
                            start=(dt == 0), stop=(dt == NDT - 1))
                assert inc(mm, "pe") == PEc[i]

        @blk.scalar
        def _(s):
            for i, (kind, b, c) in enumerate(tiles):
                if i % 2:
                    continue
                s.wait_ge(sems["pe"], PEc[i])
                w(s, "ac", cnt["ac"])
                assert inc(s.copy(kq_dst(kind, b, c), bank(i % 8)),
                           "ac") == ACc[i]

        @blk.vector
        def _(v):
            for i, (kind, b, c) in enumerate(tiles):
                if i % 2 == 0:
                    continue
                v.wait_ge(sems["pe"], PEc[i])
                c2 = dvo(v, lambda kind=kind, b=b, c=c, i=i: v.tensor_copy(
                    kq_dst(kind, b, c), bank(i % 8)))
                assert c2 == DVc[i]

        @blk.sync
        def _(sy):
            M["d_wfb"] = dma(sy, "dsy",
                             wfb[:].rearrange("p (t d) -> p t d", d=DM),
                             wf8[:, 4:6, :])
            M["d_wfc"] = dma(sy, "dsy",
                             wfc[:].rearrange("p (t d) -> p t d", d=DM),
                             wf8[:, 6:8, :])

    with nc.Block("p1_vwt") as blk:
        base_pe, base_ac, base_dv = cnt["pe"], cnt["ac"], cnt["dv"]

        def wf_t(i):
            if i < 4:
                return wfa[:, i * DM:(i + 1) * DM]
            if i < 6:
                return wfb[:, (i - 4) * DM:(i - 4 + 1) * DM]
            return wfc[:, (i - 6) * DM:(i - 6 + 1) * DM]

        PE_D = base_pe + 1
        PE_A = base_pe + 2
        PE_B = base_pe + 3
        WTR = [base_pe + 3 + i + 1 for i in range(NDT)]
        WEV = [base_ac + i + 1 for i in range(NDT)]
        DV_STAT = base_dv + 9
        DV_DSC = DV_STAT + 1
        DV_DPN = DV_DSC + 2
        DV_ABC = DV_DPN + 1
        WTQ = [DV_ABC + 3 * (i + 1) for i in range(NDT)]

        @blk.tensor
        def _(t):
            w(t, "dv", DV_STAT)
            assert inc(t.matmul(bank(6)[0:1, 0:1], FPSUM, ONESC,
                                start=True, stop=True), "pe") == PE_D
            w(t, "dv", DV_DSC)
            assert inc(t.matmul(bank(6)[:, 4:5], onesf[:], stat_sb[:, 0:1],
                                start=True, stop=True), "pe") == PE_A
            assert inc(t.matmul(bank(6)[:, 8:9], onesf[:], alpha_sb[:],
                                start=True, stop=True), "pe") == PE_B
            for i in range(NDT):
                w(t, "dv", WTQ[i])
                if i >= 2:
                    w(t, "ac", WEV[i - 2])
                bb = bankb(4 + (i % 2), 1)
                for j in range(NDT):
                    mm = t.transpose(bb[:, j * NP:(j + 1) * NP],
                                     wttmp[:, DM + j * NP: DM + (j + 1) * NP],
                                     ident[:])
                assert inc(mm, "pe") == WTR[i]

        @blk.vector
        def _(v):
            wdma(v, M["d_wfa"])
            wdma(v, M["d_wfb"])
            wdma(v, M["d_wfc"])
            for i in range(NDT):
                dvo(v, lambda i=i: v.tensor_reduce(
                    FPART[:, i:i + 1], wf_t(i),
                    AX.X, ALU.add, apply_absolute_value=True))
            c = dvo(v, lambda: v.tensor_reduce(FPSUM, FPART[:, 0:NDT],
                                               AX.X, ALU.add))
            assert c == DV_STAT
            w(v, "pe", PE_D)
            c = dvo(v, lambda: v.tensor_scalar(stat_sb[:, 0:1],
                                               bank(6)[0:1, 0:1],
                                               0.7 / float(DM * DM), None,
                                               ALU.mult))
            assert c == DV_DSC
            w(v, "pe", PE_A)
            dvo(v, lambda: v.tensor_copy(DELTAP, bank(6)[:, 4:5]))
            c = dvo(v, lambda: v.tensor_scalar(DELTAN, DELTAP, -1.0, None,
                                               ALU.mult))
            assert c == DV_DPN
            w(v, "pe", PE_B)
            c = dvo(v, lambda: v.tensor_copy(ALPHA_V, bank(6)[:, 8:9]))
            assert c == DV_ABC
            for i in range(NDT):
                wst = wf_t(i)
                dvo(v, lambda: v.tensor_scalar(wttmp[:, 0:DM], wst,
                                               DELTAP, ALPHA_V,
                                               ALU.is_gt, ALU.mult))
                dvo(v, lambda: v.tensor_scalar(wttmp2[:], wst,
                                               DELTAN, ALPHA_V,
                                               ALU.is_lt, ALU.mult))
                if i >= 1:
                    w(v, "pe", WTR[i - 1])
                c = dvo(v, lambda: v.tensor_tensor(wttmp[:, DM:2 * DM],
                                                   wttmp[:, 0:DM], wttmp2[:],
                                                   ALU.subtract))
                assert c == WTQ[i]

        @blk.scalar
        def _(s):
            for i in range(NDT):
                s.wait_ge(sems["pe"], WTR[i])
                w(s, "ac", cnt["ac"])
                bb = bankb(4 + (i % 2), 1)
                assert inc(s.copy(
                    bass.AP(wtT, i * NP, [[NDT * DM, NP], [DM, NDT], [1, NP]]),
                    bass.AP(psb, bb.offset, [[8 * 1024, NP], [NP, NDT], [1, NP]])),
                    "ac") == WEV[i]

    ST = [bank(0, 2), bank(2, 2)]
    PV = bank(4, 2)
    DEN = bank(6, 2)
    base = {s: cnt[s] for s in ("sc0", "sc1", "ex0", "ex1", "pvs", "eps")}
    assert all(v == 0 for v in base.values())

    def sp_decode(sp):
        sqc, p = divmod(sp, NPAIR)
        return sqc, p

    def emit_pv_step(t, g, start, stop, tail_split=False, eps_sp=None):
        sp, skt = divmod(g, NST)
        sqc, p = sp_decode(sp)
        st_of = [v_sl(skt, (2 * p + h) * HD, HD) for h in range(2)]
        for c in range(2):
            if eps_sp is not None:
                w(t, "eps", 2 * eps_sp - 1 + c)
            rh0 = pring(0)[:, (g % 4) * SQW + c * 512:(g % 4) * SQW + c * 512 + 512]
            rh1 = pring(1)[:, (g % 4) * SQW + c * 512:(g % 4) * SQW + c * 512 + 512]
            t.matmul(PV[0:64, c * 512:(c + 1) * 512], st_of[0], rh0,
                     start=start, stop=stop, tile_position=(0, 0))
            t.matmul(PV[64:128, c * 512:(c + 1) * 512], st_of[1], rh1,
                     start=start, stop=stop, tile_position=(0, 64))
            t.matmul(DEN[0:64, c * 512:(c + 1) * 512], ones_den[:], rh0,
                     start=start, stop=stop, tile_position=(0, 0))
            mm = t.matmul(DEN[64:128, c * 512:(c + 1) * 512], ones_den[:], rh1,
                          start=start, stop=stop, tile_position=(0, 64))
            if tail_split and c == 0:
                inc(mm, "pvt")
        inc(mm, "pvs")

    with nc.Block("p2") as blk:
        @blk.tensor
        def _(t):
            for sp in range(NSP):
                sqc, p = sp_decode(sp)
                for skt in range(NST):
                    g = sp * NST + skt
                    w(t, "ex0", g)
                    for c in range(2):
                        col = sqc * SQW + c * 512
                        mm0 = t.matmul(
                            ST[0][:, c * 512:(c + 1) * 512],
                            kT[0:64, p * SEQ + skt * NP: p * SEQ + (skt + 1) * NP],
                            qT_sl(p, 0, 64, col, 512),
                            start=True, stop=True, tile_position=(0, 0))
                    inc(mm0, "sc0")
                    w(t, "ex1", g)
                    for c in range(2):
                        col = sqc * SQW + c * 512
                        mm1 = t.matmul(
                            ST[1][:, c * 512:(c + 1) * 512],
                            kT[64:128, p * SEQ + skt * NP: p * SEQ + (skt + 1) * NP],
                            qT_sl(p, 64, 64, col, 512),
                            start=True, stop=True, tile_position=(64, 0))
                    inc(mm1, "sc1")
                    if skt == 0 and sp > 0:
                        gp_ = sp * NST - 1
                        w(t, "ex0", gp_ + 1)
                        w(t, "ex1", gp_ + 1)
                        emit_pv_step(t, gp_, start=False, stop=True,
                                     tail_split=True)
                    if skt > 0:
                        emit_pv_step(t, g - 1, start=(skt - 1 == 0),
                                     stop=(skt - 1 == NST - 1 and sp == NSP - 1),
                                     eps_sp=(sp if skt == 1 and sp > 0 else None))
                if sp == NSP - 1:
                    g_last = sp * NST + NST - 1
                    w(t, "ex0", g_last + 1)
                    w(t, "ex1", g_last + 1)
                    emit_pv_step(t, g_last, start=False, stop=True,
                                 tail_split=True)

        @blk.scalar
        def _(s):
            for sp in range(NSP):
                for skt in range(NST):
                    g = sp * NST + skt
                    w(s, "sc0", g + 1)
                    if g >= 4 and g % 2 == 0:
                        w(s, "pvs", g - 2)
                    inc(s.activation(
                        pring(0)[:, (g % 4) * SQW:(g % 4 + 1) * SQW],
                        ST[0], AF.Exp, scale=0.125), "ex0")
                    w(s, "sc1", g + 1)
                    inc(s.activation(
                        pring(1)[:, (g % 4) * SQW:(g % 4 + 1) * SQW],
                        ST[1], AF.Exp, scale=0.125), "ex1")

        @blk.vector
        def _(v):
            for sp in range(NSP):
                sqc, p = sp_decode(sp)
                w(v, "pvt", sp + 1)
                w(v, "dv", cnt["dv"])
                inc(v.tensor_copy(pvden_stage[:, 0:512], bank(4, 1)), "dv")
                inc(v.tensor_copy(pvden_stage[:, SQW:SQW + 512], bank(6, 1)),
                    "eps")
                w(v, "pvs", (sp + 1) * NST)
                inc(v.tensor_copy(pvden_stage[:, 512:SQW], bank(5, 1)), "dv")
                inc(v.tensor_copy(pvden_stage[:, SQW + 512:2 * SQW],
                                  bank(7, 1)), "eps")
                w(v, "eps", 2 * (sp + 1))
                dvo(v, lambda: v.reciprocal(rd_bc[:], pvden_stage[:, SQW:]))
                dvo(v, lambda sp=sp, sqc=sqc, p=p: v.tensor_tensor(
                    OT[:, p * SEQ + sqc * SQW: p * SEQ + (sqc + 1) * SQW],
                    pvden_stage[:, 0:SQW], rd_bc[:], ALU.mult))

    def lnv3(st, k):
        return vec[:, 16 + (st % 3) * 8 + k: 16 + (st % 3) * 8 + k + 1]

    with nc.Block("p3") as blk:
        P3VAR = [None] * NST
        MIDC = [None] * NST
        SQC = [None] * NST
        REC = [None] * NST
        TEVC3 = [None] * NST
        LNRC = [None] * NST
        WOC = [None] * NST
        TRC3 = [None] * NST
        basedv, baseac = cnt["dv"], cnt["ac"]
        _ac = baseac
        _orda = []
        for i in range(NST):
            if i >= 3:
                _orda.append(("cs", i - 3))
            _orda.append(("sq", i))
            if i >= 1:
                _orda.append(("le", i - 1))
            if i >= 4:
                _orda.append(("tv", i - 4))
        _orda += [("le", NST - 1), ("cs", NST - 3), ("cs", NST - 2),
                  ("cs", NST - 1), ("tv", NST - 4), ("tv", NST - 3),
                  ("tv", NST - 2), ("tv", NST - 1)]
        POSTC = [None] * NST
        _lnbf = 0
        for kind, j in _orda:
            if kind == "sq":
                _ac += 1
                SQC[j] = _ac
            elif kind == "le":
                _ac += 2
                REC[j] = _ac
            elif kind == "cs":
                _lnbf += 1
                POSTC[j] = _lnbf
            else:
                _ac += 2
                TEVC3[j] = _ac
        _dv = basedv
        _lnr = 0
        _ordd = []
        for i in range(NST):
            _ordd.append(("pre", i))
            if i >= 1:
                _ordd.append(("mid", i - 1))
            if i >= 2:
                _ordd.append(("post", i - 2))
        _ordd += [("mid", NST - 1), ("post", NST - 2), ("post", NST - 1)]
        PSTC = [None] * NST
        for kind, j in _ordd:
            if kind == "pre":
                _lnr += 1
                LNRC[j] = _lnr
            elif kind == "mid":
                _dv += 4
                MIDC[j] = _dv
            else:
                _dv += 3
                PSTC[j] = _dv
        _wo = 0
        _tr = 0
        _ordp = []
        for i in range(NST):
            _ordp.append(("wo", i))
            if i >= 3:
                _ordp.append(("tr", i - 3))
        _ordp += [("tr", NST - 3), ("tr", NST - 2), ("tr", NST - 1)]
        for kind, j in _ordp:
            if kind == "wo":
                _wo += 1
                WOC[j] = _wo
            else:
                _tr += 1
                TRC3[j] = _tr

        def p3_pair(st):
            return bank(2 * (st % 3), 2)

        def rsl(st):
            return rtile[:, 0:DM] if False else None

        @blk.sync
        def _(sy):
            M["d_xr_t"] = []
            for st in range(NST):
                if st >= 2:
                    sy.wait_ge(sems["lnr"], st - 1)
                M["d_xr_t"].append(dma(
                    sy, "dsy",
                    xres[:, (st % 2) * DM:(st % 2) * DM + DM]
                    .rearrange("p (o d) -> p o d", o=1),
                    x_h.ap().rearrange("(t p) d -> p t d", p=NP)[:, st:st + 1, :]))

        @blk.tensor
        def _(t):
            def emit_wo(st):
                if st >= 3:
                    w(t, "lnr", LNRC[st - 3])
                for oc in range(2):
                    for p in range(NPAIR):
                        mm = t.matmul(
                            p3_pair(st)[:, oc * 512:(oc + 1) * 512],
                            OT[:, p * SEQ + st * NP: p * SEQ + (st + 1) * NP],
                            woT[:, p * DM + oc * 512: p * DM + oc * 512 + 512],
                            start=(p == 0), stop=(p == NPAIR - 1))
                assert inc(mm, "womm") == WOC[st]

            def emit_tr(st):
                w(t, "lnbf", POSTC[st])
                if st >= 2:
                    w(t, "ac", TEVC3[st - 2])
                bb = bankb(6 + (st % 2), 1)
                for j in range(NDT):
                    mm = t.transpose(
                        bb[:, j * NP:(j + 1) * NP],
                        x1bfw[:, (st % 2) * DM + j * NP:
                              (st % 2) * DM + (j + 1) * NP],
                        ident[:])
                assert inc(mm, "x1tr") == TRC3[st]

            for i in range(NST):
                emit_wo(i)
                if i >= 3:
                    emit_tr(i - 3)
            for j in (NST - 3, NST - 2, NST - 1):
                emit_tr(j)

        @blk.vector
        def _(v):
            def pre(st):
                w(v, "womm", WOC[st])
                wdma(v, M["d_xr_t"][st])
                if st >= 3:
                    w(v, "ac", SQC[st - 3])
                r = rtile[:, (st % 3) * DM:(st % 3) * DM + DM]
                w(v, "dv", cnt["dv"])
                assert inc(v.scalar_tensor_tensor(
                    r, p3_pair(st), 1.0,
                    xres[:, (st % 2) * DM:(st % 2) * DM + DM],
                    ALU.mult, ALU.add, accum_out=lnv3(st, 0)),
                    "lnr") == LNRC[st]
                w(v, "lnr", LNRC[st])

            def mid(st):
                v.wait_ge(sems["ac"], SQC[st])
                dvo(v, lambda: v.tensor_scalar(lnv3(st, 2), lnv3(st, 0),
                                               1.0 / DM, None, ALU.mult))
                dvo(v, lambda: v.tensor_scalar(lnv3(st, 3), lnv3(st, 1),
                                               1.0 / DM, EPS, ALU.mult, ALU.add))
                dvo(v, lambda: v.tensor_tensor(lnv3(st, 4), lnv3(st, 2),
                                               lnv3(st, 2), ALU.mult))
                c = dvo(v, lambda: v.tensor_tensor(lnv3(st, 3), lnv3(st, 3),
                                                   lnv3(st, 4), ALU.subtract))
                assert c == MIDC[st]
                P3VAR[st] = c

            def post(st):
                v.wait_ge(sems["ac"], REC[st])
                r = rtile[:, (st % 3) * DM:(st % 3) * DM + DM]
                dvo(v, lambda: v.tensor_tensor(lnv3(st, 6), lnv3(st, 2),
                                               lnv3(st, 5), ALU.mult))
                dvo(v, lambda: v.tensor_scalar(lnv3(st, 6), lnv3(st, 6),
                                               -1.0, None, ALU.mult))
                c = dvo(v, lambda: v.tensor_scalar(x1_t(st), r, lnv3(st, 5),
                                                   lnv3(st, 6), ALU.mult,
                                                   ALU.add))
                assert c == PSTC[st]

            for i in range(NST):
                pre(i)
                if i >= 1:
                    mid(i - 1)
                if i >= 2:
                    post(i - 2)
            mid(NST - 1)
            post(NST - 2)
            post(NST - 1)

        @blk.scalar
        def _(s):
            sqd = sqs[:, 0:DM]

            def sq(st):
                s.wait_ge(sems["lnr"], LNRC[st])
                if st >= 3:
                    s.wait_ge(sems["dv"], MIDC[st - 3])
                r = rtile[:, (st % 3) * DM:(st % 3) * DM + DM]
                w(s, "ac", cnt["ac"])
                assert inc(s.activation(sqd, r, AF.Square,
                                        accum_out=lnv3(st, 1)), "ac") == SQC[st]

            def lnexp(st):
                s.wait_ge(sems["dv"], MIDC[st])
                w(s, "ac", cnt["ac"])
                inc(s.activation(lnv3(st, 5), lnv3(st, 3), AF.Ln), "ac")
                w(s, "ac", cnt["ac"])
                assert inc(s.activation(lnv3(st, 5), lnv3(st, 5), AF.Exp,
                                        scale=-0.5), "ac") == REC[st]

            def cast(st):
                s.wait_ge(sems["dv"], PSTC[st])
                if st >= 2:
                    s.wait_ge(sems["x1tr"], TRC3[st - 2])
                w(s, "ac", cnt["ac"])
                assert inc(s.copy(
                    x1bfw[:, (st % 2) * DM:(st % 2) * DM + DM], x1_t(st)),
                    "lnbf") == POSTC[st]

            def tev(st):
                s.wait_ge(sems["x1tr"], TRC3[st])
                bb = bankb(6 + (st % 2), 1)
                col = st * NP
                w(s, "ac", cnt["ac"])
                inc(s.copy(
                    bass.AP(x1Ta, col, [[4 * SEQ, NP], [SEQ, 4], [1, NP]]),
                    bass.AP(psb, bb.offset, [[8 * 1024, NP], [NP, 4], [1, NP]])),
                    "ac")
                w(s, "ac", cnt["ac"])
                assert inc(s.copy(
                    bass.AP(x1Tb, col, [[4 * SEQ, NP], [SEQ, 4], [1, NP]]),
                    bass.AP(psb, bb.offset + 4 * NP,
                            [[8 * 1024, NP], [NP, 4], [1, NP]])),
                    "ac") == TEVC3[st]

            for i in range(NST):
                if i >= 3:
                    cast(i - 3)
                sq(i)
                if i >= 1:
                    lnexp(i - 1)
                if i >= 4:
                    tev(i - 4)
            lnexp(NST - 1)
            for j in (NST - 3, NST - 2, NST - 1):
                cast(j)
            for j in (NST - 4, NST - 3, NST - 2, NST - 1):
                tev(j)

    with nc.Block("p4") as blk:
        P4VAR = [None] * NST
        P4PST = [None] * NST
        MID4 = [None] * NST
        SQ4 = [None] * NST
        RE4 = [None] * NST
        FFC = [None] * NST
        basedv4, baseac4 = cnt["dv"], cnt["ac"]
        _ac = baseac4
        for i in range(NST):
            _ac += 1
            SQ4[i] = _ac
            if i >= 1:
                _ac += 2
                RE4[i - 1] = _ac
        RE4[NST - 1] = _ac + 2
        _rr = dma_rr["dsy"]
        _pc = {i: cnt[f"dsy{i}"] for i in range(NDMASEM)}
        M["d_out_t"] = []
        for st in range(NST):
            i = (_rr + st) % NDMASEM
            _pc[i] += 16
            M["d_out_t"].append((f"dsy{i}", _pc[i]))

        r2ring = place("r2ring", [NP, 3 * DM], F32, 150 * KB)
        sqd4 = place("sqd4", [NP, DM], F32, 162 * KB)
        lnv4t = lnv3

        def r2sl(st):
            return r2ring[:, (st % 3) * DM:(st % 3) * DM + DM]

        @blk.tensor
        def _(t):
            wdma(t, M["d_blo"])
            for st in range(NST):
                if st >= 4:
                    w(t, "ffnr", st - 3)
                for oc in range(2):
                    t.matmul(bank(2 * (st % 4) + oc), ones2[:],
                             bias_hl[:, oc * 512:(oc + 1) * 512],
                             start=True, stop=False)
                    for dt in range(NDT):
                        xs = x1Ta if dt < 4 else x1Tb
                        mm = t.matmul(
                            bank(2 * (st % 4) + oc),
                            xs[:, (dt % 4) * SEQ + st * NP:
                               (dt % 4) * SEQ + (st + 1) * NP],
                            wtT[:, dt * DM + oc * 512: dt * DM + oc * 512 + 512],
                            start=False, stop=(dt == NDT - 1))
                inc(mm, "ffnm")

        @blk.vector
        def _(v):
            def pre(st):
                w(v, "ffnm", st + 1)
                if st >= 3:
                    w(v, "ac", SQ4[st - 3])
                    w(v, "ln2ap", st - 2)
                r2 = r2sl(st)
                w(v, "dv", cnt["dv"])
                inc(v.scalar_tensor_tensor(
                    r2, bank(2 * (st % 4), 2), 1.0,
                    x1_t(st), ALU.mult, ALU.add, accum_out=lnv4t(st, 0)),
                    "ffnr")
                w(v, "ffnr", st + 1)

            def mid(st):
                v.wait_ge(sems["ac"], SQ4[st])
                dvo(v, lambda: v.tensor_scalar(lnv4t(st, 2), lnv4t(st, 0),
                                               1.0 / DM, None, ALU.mult))
                dvo(v, lambda: v.tensor_scalar(lnv4t(st, 3), lnv4t(st, 1),
                                               1.0 / DM, EPS, ALU.mult, ALU.add))
                dvo(v, lambda: v.tensor_scalar(lnv4t(st, 4), lnv4t(st, 2),
                                               lnv4t(st, 2), None, ALU.mult))
                P4VAR[st] = dvo(v, lambda: v.tensor_tensor(
                    lnv4t(st, 3), lnv4t(st, 3), lnv4t(st, 4), ALU.subtract))
                MID4[st] = P4VAR[st]

            def post(st):
                v.wait_ge(sems["ac"], RE4[st])
                r2 = r2sl(st)
                dvo(v, lambda: v.tensor_tensor(lnv4t(st, 6), lnv4t(st, 2),
                                               lnv4t(st, 5), ALU.mult))
                dvo(v, lambda: v.tensor_scalar(lnv4t(st, 6), lnv4t(st, 6),
                                               -1.0, None, ALU.mult))
                if st >= 2:
                    wdma(v, M["d_out_t"][st - 2])
                w(v, "dv", cnt["dv"])
                inc(v.tensor_scalar(
                    ostage[:, (st % 2) * DM:(st % 2) * DM + DM],
                    r2, lnv4t(st, 5), lnv4t(st, 6), ALU.mult, ALU.add), "ln2ap")

            for i in range(NST):
                pre(i)
                if i >= 1:
                    mid(i - 1)
                if i >= 2:
                    post(i - 2)
            mid(NST - 1)
            post(NST - 2)
            post(NST - 1)

        @blk.scalar
        def _(s):
            def sq(st):
                s.wait_ge(sems["ffnr"], st + 1)
                if st >= 3:
                    s.wait_ge(sems["dv"], MID4[st - 3])
                w(s, "ac", cnt["ac"])
                assert inc(s.activation(sqd4[:], r2sl(st), AF.Square,
                                        accum_out=lnv4t(st, 1)), "ac") == SQ4[st]

            def lnexp(st):
                s.wait_ge(sems["dv"], P4VAR[st])
                w(s, "ac", cnt["ac"])
                inc(s.activation(lnv4t(st, 5), lnv4t(st, 3), AF.Ln), "ac")
                w(s, "ac", cnt["ac"])
                assert inc(s.activation(lnv4t(st, 5), lnv4t(st, 5), AF.Exp,
                                        scale=-0.5), "ac") == RE4[st]

            for i in range(NST):
                sq(i)
                if i >= 1:
                    lnexp(i - 1)
            lnexp(NST - 1)

        @blk.sync
        def _(sy):
            for st in range(NST):
                sy.wait_ge(sems["ln2ap"], st + 1)
                tok = dma(
                    sy, "dsy",
                    out_h.ap().rearrange("(t p) d -> p t d", p=NP)[:, st:st + 1, :],
                    ostage[:, (st % 2) * DM:(st % 2) * DM + DM]
                    .rearrange("p (o d) -> p o d", o=1))
                assert tok == M["d_out_t"][st], (tok, M["d_out_t"][st])
            for i in range(NDMASEM):
                sy.wait_ge(sems[f"dsy{i}"], cnt[f"dsy{i}"])

    return nc


_PROG_CACHE = {}


def _get_program():
    if "nc" not in _PROG_CACHE:
        _PROG_CACHE["nc"] = build_program()
    return _PROG_CACHE["nc"]


def _rot_matrices():
    inv = 1.0 / (10000.0 ** (np.arange(0, HD, 2, dtype=np.float64)[: DROPE // 2] / HD))
    R = np.zeros((NP, NPAIR * NP), dtype=np.float64)
    for pb in range(NPAIR):
        blk = np.eye(NP, dtype=np.float64)
        for j in range(2):
            h = 2 * pb + j
            for k in range(DROPE // 2):
                c, s = np.cos(h * inv[k]), np.sin(h * inv[k])
                e, o = j * HD + 2 * k, j * HD + 2 * k + 1
                blk[e, e] = c
                blk[o, e] = -s
                blk[e, o] = s
                blk[o, o] = c
        R[:, pb * NP:(pb + 1) * NP] = blk
    hi = R.astype(ml_dtypes.bfloat16)
    lo = (R - hi.astype(np.float64)).astype(ml_dtypes.bfloat16)
    return np.ascontiguousarray(hi), np.ascontiguousarray(lo)


def kernel(L0_attn_x, L0_attn_wq, L0_attn_wk_a, L0_attn_wk_b, L0_attn_wv_a,
           L0_attn_wv_b, L0_attn_wo, L0_ffn_weight, L0_ffn_alpha, L0_ffn_bias):
    nc = _get_program()
    rhi, rlo = _rot_matrices()
    ident = np.eye(NP, dtype=ml_dtypes.bfloat16)
    common = {
        "wq": np.ascontiguousarray(L0_attn_wq, dtype=np.float32),
        "wka": np.ascontiguousarray(L0_attn_wk_a, dtype=np.float32),
        "wkb": np.ascontiguousarray(L0_attn_wk_b, dtype=np.float32),
        "wva": np.ascontiguousarray(L0_attn_wv_a, dtype=np.float32),
        "wvb": np.ascontiguousarray(L0_attn_wv_b, dtype=np.float32),
        "wo": np.ascontiguousarray(L0_attn_wo, dtype=np.float32),
        "wf": np.ascontiguousarray(L0_ffn_weight, dtype=np.float32),
        "alpha": np.asarray(L0_ffn_alpha, dtype=np.float32).reshape(1, 1),
        "fbias": np.asarray(L0_ffn_bias, dtype=np.float32).reshape(1, DM),
        "identi": ident,
        "rhi": rhi, "rlo": rlo,
    }
    x = np.asarray(L0_attn_x, dtype=np.float32)
    in_maps = [dict(common, x=np.ascontiguousarray(x[c])) for c in range(8)]
    res = run_bass_kernel_spmd(nc, in_maps, core_ids=list(range(8)))
    out = np.stack([np.asarray(res.results[c]["out"]) for c in range(8)], axis=0)
    return out.astype(np.float32)
